# version 29
# baseline (speedup 1.0000x reference)
import numpy as np
import concourse.bass as bass
import concourse.mybir as mybir
from concourse.bass_utils import run_bass_kernel_spmd

F32 = mybir.dt.float32
BF16 = mybir.dt.bfloat16
AF = mybir.ActivationFunctionType
ALU = mybir.AluOpType
AX = mybir.AxisListType

NEG_BIG = -1.0e30
LN_EPS = 1.0e-5


def default_cfg():
    return dict(S=2048, D=2048, H=16, DFF=8192, G=1, SC=512, QC=512, EC=512,
                n_cores=8)


class Buf:
    __slots__ = ("w", "r", "dsem")

    def __init__(self, init=None):
        self.w = dict(init) if init else {}
        self.r = dict(init) if init else {}
        self.dsem = None


class Sched:
    COMPUTE_SEM = {"tensor": "T", "vector": "V", "scalar": "A", "gpsimd": "G"}
    QUEUE_SEM = {"sync": "qS", "scalar": "qA", "gpsimd": "qG"}
    ENGINES = ("tensor", "vector", "scalar", "gpsimd", "sync")

    def __init__(self, nc):
        self.nc = nc
        self.ops = {e: [] for e in self.ENGINES}
        self.sem = {}
        self.cnt = {}
        for s in ("T", "V", "A", "G", "qS", "qA", "qG"):
            self.sem[s] = nc.alloc_semaphore("sem_" + s)
            self.cnt[s] = 0

    def snap(self):
        return dict(self.cnt)

    def buf(self):
        return Buf(getattr(self, "_snap", None))

    def set_barrier(self):
        self._snap = dict(self.cnt)

    def op(self, engine, fn, reads=(), writes=(), writes_nodep=(),
           war=(), dma=False):
        need = {}
        for b in reads:
            for s, v in b.w.items():
                if need.get(s, 0) < v:
                    need[s] = v
        for b in war:
            for s, v in b.r.items():
                if need.get(s, 0) < v:
                    need[s] = v
        for b in writes:
            for d in (b.w, b.r):
                for s, v in d.items():
                    if need.get(s, 0) < v:
                        need[s] = v
        if dma:
            tb = (list(writes) + list(writes_nodep))[0]
            if tb.dsem is None:
                tb.dsem = f"qd{len(self.sem)}"
                self.sem[tb.dsem] = self.nc.alloc_semaphore(tb.dsem)
                self.cnt[tb.dsem] = 0
            s = tb.dsem
        else:
            s = self.COMPUTE_SEM[engine]
        amt = 16 if dma else 1
        self.cnt[s] += amt
        tok = (s, self.cnt[s])
        self.ops[engine].append((need, fn, s, amt))
        for b in list(writes) + list(writes_nodep):
            if b.w.get(s, 0) < tok[1]:
                b.w[s] = tok[1]
        for b in reads:
            if b.r.get(s, 0) < tok[1]:
                b.r[s] = tok[1]
        return tok

    def finalize(self):
        final = dict(self.cnt)
        self.ops["sync"].append((final, None, None, 0))

    RENUM = ("T", "V", "A")

    def plan(self):
        needed = {s: set() for s in self.RENUM}
        for engine in self.ENGINES:
            wm = {}
            for need, fn, s, amt in self.ops[engine]:
                for ns in sorted(need):
                    v = need[ns]
                    if wm.get(ns, 0) < v:
                        if ns in needed:
                            needed[ns].add(v)
                        wm[ns] = v
        for s in self.RENUM:
            if self.cnt[s] > 0:
                needed[s].add(self.cnt[s])
        remap = {}
        for s in self.RENUM:
            vals = sorted(needed[s])
            m = {v: k + 1 for k, v in enumerate(vals)}
            remap[s] = (needed[s], m)
        return remap

    def emit(self):
        nc = self.nc
        remap = self.plan()

        def map_thr(s, v):
            if s not in remap:
                return v
            return remap[s][1][v]

        with nc.Block() as block:
            def runner(engine):
                oplist = self.ops[engine]
                sems = self.sem
                esem = self.COMPUTE_SEM.get(engine)

                def go(eng):
                    wm = {}
                    tok = 0
                    for need, fn, s, amt in oplist:
                        for ns in sorted(need):
                            v = need[ns]
                            if wm.get(ns, 0) < v:
                                eng.wait_ge(sems[ns], map_thr(ns, v))
                                wm[ns] = v
                        if fn is not None:
                            inst = fn(eng)
                            if s == esem and s in remap:
                                tok += 1
                                if tok in remap[s][0]:
                                    inst.then_inc(sems[s], amt)
                            else:
                                inst.then_inc(sems[s], amt)
                return go

            block.tensor(runner("tensor"))
            block.vector(runner("vector"))
            block.scalar(runner("scalar"))
            block.gpsimd(runner("gpsimd"))
            block.sync(runner("sync"))


class Arena:

    def __init__(self, nc, base, limit, tag):
        self.nc = nc
        self.off = base
        self.limit = limit
        self.tag = tag
        self.n = 0

    def alloc(self, name, shape, dtype):
        nbytes = shape[1] * mybir.dt.size(dtype)
        nbytes = (nbytes + 31) & ~31
        assert self.off + nbytes <= self.limit, (
            f"{self.tag}/{name}: {self.off}+{nbytes} > {self.limit}")
        h = self.nc.alloc_sbuf_tensor_at(
            f"{self.tag}_{self.n}_{name}", shape, dtype, offset=self.off)
        self.n += 1
        self.off += nbytes
        return h


def t128(i):
    return slice(i * 128, (i + 1) * 128)


def build_program(cfg):
    S, D, H, DFF = cfg["S"], cfg["D"], cfg["H"], cfg["DFF"]
    G, SC, QC, EC = cfg["G"], cfg["SC"], cfg["QC"], cfg["EC"]
    HD = 128
    assert D == H * HD
    assert G == 1, "projection feeder assumes one head per group"
    NTS, ND, NF = S // 128, D // 128, DFF // 128
    NCH = S // SC
    QT = QC // 128
    NQC = S // QC
    NEC = D // EC
    FC = min(4, NF)
    scale = HD ** -0.5
    MAXPH = cfg.get("max_phase", 99)
    P0DBG = cfg.get("p0_dbg", "LTSW")

    nc = bass.Bass("TRN2", target_bir_lowering=False)

    d_x = nc.dram_tensor("x", [S, D], F32, kind="ExternalInput").ap()
    d_wqkv = nc.dram_tensor("wqkv", [H * 128, 3 * ND * 128], F32,
                            kind="ExternalInput").ap()
    d_wout = nc.dram_tensor("woutT", [D, D], F32, kind="ExternalInput").ap()
    d_w1p = nc.dram_tensor("w1p", [NF * 128, D], F32,
                           kind="ExternalInput").ap()
    d_w2t = nc.dram_tensor("w2t", [DFF, D], F32, kind="ExternalInput").ap()
    d_b1c = nc.dram_tensor("b1c", [128, NF], F32, kind="ExternalInput").ap()
    d_b2r = nc.dram_tensor("b2r", [1, D], F32, kind="ExternalInput").ap()
    d_ident = nc.dram_tensor("identc", [128, 128], F32,
                             kind="ExternalInput").ap()
    d_mask = nc.dram_tensor("maskneg", [128, 128], F32,
                            kind="ExternalInput").ap()
    d_out = nc.dram_tensor("out", [S, D], F32, kind="ExternalOutput").ap()
    d_xn = [nc.dram_tensor(f"xn_{p}", [NCH * 128, ND * SC], BF16).ap()
            for p in range(2)]
    d_y = [nc.dram_tensor(f"y_{p}", [D, S], BF16).ap() for p in range(2)]
    d_g = [nc.dram_tensor(f"g_{p}", [DFF, S], BF16).ap() for p in range(2)]

    SBW = min(S, 2048)
    ps_big = nc.alloc_psum_tensor("ps_big", [128, 2048], F32)
    ps_tr = nc.alloc_psum_tensor("ps_tr", [128, 1024], F32)
    ps_m = [nc.alloc_psum_tensor(f"ps_m{i}", [128, 512], F32) for i in (0, 1)]

    sch = Sched(nc)
    b_big = sch.buf()
    b_sb4 = [sch.buf() for _ in range(4)]
    sb_cnt = [0]
    b_tr = [sch.buf(), sch.buf()]
    b_m = [sch.buf(), sch.buf()]
    trc = [0]

    def tr_next():
        k = trc[0]
        trc[0] += 1
        b = k % 2
        return ps_tr[:, b * 512:b * 512 + 128], b_tr[b]

    SB_LIMIT = 229376 - 32
    ARENA_BASE = 17408
    ACC_BYTES = NTS * D * 4
    ACC_OFF = SB_LIMIT - ACC_BYTES
    pa = Arena(nc, ARENA_BASE, SB_LIMIT, "pers")
    ident = pa.alloc("ident", [128, 128], F32)
    mask = pa.alloc("mask", [128, 128], F32)
    ones = pa.alloc("ones", [1, 128], F32)
    b1c = pa.alloc("b1c", [128, max(NF, 8)], F32)
    thr = pa.alloc("thr", [128, 8], F32)
    sv = pa.alloc("sv", [128, 16], F32)
    cst = pa.alloc("cst", [128, 2], F32)
    st6 = pa.alloc("st6", [128, 6 * max((D + 511) // 512, 1)], F32)
    mv = pa.alloc("mv", [128, 2], F32)
    PERS_END = pa.off
    ppa = Arena(nc, PERS_END, SB_LIMIT, "pp")
    ppart = ppa.alloc("ppart",
                      [128, max(NF * D // 128, DFF * D // 16384, 128)], F32)
    pp2 = ppa.alloc("pp2", [128, 16], F32)
    PPW = min(512, D)
    ppw = [ppa.alloc(f"ppw{i}", [128, PPW], F32) for i in range(5)]
    PP_END = ppa.off
    assert PP_END < ACC_OFF

    b_ident, b_mask, b_ones, b_b1c = sch.buf(), sch.buf(), sch.buf(), sch.buf()
    b_thr = sch.buf()
    b_sv = sch.buf()
    b_cst = sch.buf()
    b_st6, b_mv = sch.buf(), sch.buf()
    b_ppart, b_pp2 = sch.buf(), sch.buf()
    b_ppw = [sch.buf() for _ in range(5)]
    b_ppwall = sch.buf()
    b_acc = None
    b_xn_dram = [[sch.buf() for _ in range(max(S // SC, 1))]
                 for _ in range(2)]
    b_y_dram = [sch.buf(), sch.buf()]
    b_g_dram = [sch.buf(), sch.buf()]
    b_out_dram = sch.buf()

    sch.op("sync", lambda e: e.dma_start(ident[:, :], d_ident),
           writes=[b_ident], dma=True)
    sch.op("sync", lambda e: e.dma_start(mask[:, :], d_mask),
           writes=[b_mask], dma=True)
    sch.op("sync", lambda e: e.dma_start(b1c[:, 0:NF], d_b1c),
           writes=[b_b1c], dma=True)
    sch.op("vector", lambda e: e.memset(ones[0:1, :], 1.0), writes=[b_ones])
    sch.op("vector", lambda e: e.memset(cst[:, 0:1], LN_EPS), writes=[b_cst])
    sch.op("vector", lambda e: e.memset(cst[:, 1:2], scale), writes=[b_cst])

    def split_psum(ps_ap, hi_ap, lo_ap, ps_buf, hi_buf, lo_buf):
        sch.op("scalar", lambda e, a=ps_ap, b=hi_ap: e.copy(b, a),
               reads=[ps_buf], writes=[hi_buf])
        sch.op("vector",
               lambda e, a=ps_ap, b=hi_ap, c=lo_ap:
               e.scalar_tensor_tensor(c, a, 1.0, b, ALU.mult, ALU.subtract),
               reads=[ps_buf, hi_buf], writes=[lo_buf])

    def split_psum_v(ps_ap, hi_ap, lo_ap, ps_buf, hi_buf, lo_buf):
        sch.op("vector", lambda e, a=ps_ap, b=hi_ap: e.tensor_copy(b, a),
               reads=[ps_buf], writes=[hi_buf])
        sch.op("vector",
               lambda e, a=ps_ap, b=hi_ap, c=lo_ap:
               e.scalar_tensor_tensor(c, a, 1.0, b, ALU.mult, ALU.subtract),
               reads=[ps_buf, hi_buf], writes=[lo_buf])

    def prepass_stream(d_w, rows):
        nblk = rows // 128
        npc = D // PPW
        pcw = PPW // 128
        chunks = [(r, q) for r in range(nblk) for q in range(npc)]
        state = {"i": 0}

        def step(k):
            i0 = state["i"]
            for idx in range(i0, min(i0 + k, len(chunks))):
                r, q = chunks[idx]
                i = idx % 5
                sch.op("sync",
                       lambda e, r=r, q=q, i=i: e.dma_start(
                           ppw[i][:, :], d_w[t128(r), q * PPW:(q + 1) * PPW]),
                       war=[b_ppw[i]], writes_nodep=[b_ppw[i]],
                       dma=True)
                c0 = r * npc + q
                sch.op("scalar",
                       lambda e, c0=c0, i=i: e.activation(
                           ppw[i][:, :], ppw[i][:, :], AF.Abs,
                           accum_out=ppart[:, c0:c0 + 1]),
                       reads=[b_ppw[i]], writes=[b_ppw[i], b_ppart])
            state["i"] = min(i0 + k, len(chunks))

        def flush():
            step(len(chunks))
        return step, flush, nblk * npc

    def prepass_fin(tot, thr_col, nelem):
        k1 = 1
        while tot // k1 > 128:
            k1 *= 2
        sch.op("vector",
               lambda e, tot=tot, k1=k1: e.tensor_reduce(
                   pp2[:, 0:k1],
                   ppart[:, 0:tot].rearrange("p (k c) -> p k c", k=k1),
                   AX.X, ALU.add, False),
               reads=[b_ppart], writes=[b_pp2])
        sch.op("vector",
               lambda e, k1=k1, thr_col=thr_col: e.tensor_reduce(
                   pp2[:, 8 + thr_col:9 + thr_col],
                   pp2[:, 0:k1], AX.X, ALU.add, False),
               reads=[b_pp2], writes=[b_pp2])
        ta, tb = tr_next()
        sch.op("tensor",
               lambda e, ta=ta, thr_col=thr_col: e.transpose(
                   ta[0:1, :], pp2[:, 8 + thr_col:9 + thr_col], ident[:, :]),
               reads=[b_pp2, b_ident], writes=[tb])
        sch.op("vector",
               lambda e, ta=ta, thr_col=thr_col: e.tensor_reduce(
                   pp2[0:1, 12 + thr_col:13 + thr_col],
                   ta[0:1, :], AX.X, ALU.add, False),
               reads=[tb], writes=[b_pp2])
        tc, tcb = tr_next()
        sch.op("tensor",
               lambda e, tc=tc, thr_col=thr_col: e.matmul(
                   tc[:, 0:1],
                   ones[0:1, 0:128], pp2[0:1, 12 + thr_col:13 + thr_col],
                   start=True, stop=True),
               reads=[b_pp2, b_ones], writes=[tcb])
        c = 0.7 / float(nelem)
        sch.op("vector",
               lambda e, tc=tc, thr_col=thr_col, c=c: e.tensor_scalar_mul(
                   thr[:, 2 * thr_col:2 * thr_col + 1], tc[:, 0:1], c),
               reads=[tcb], writes=[b_thr])
        sch.op("vector",
               lambda e, tc=tc, thr_col=thr_col, c=c: e.tensor_scalar_mul(
                   thr[:, 2 * thr_col + 1:2 * thr_col + 2], tc[:, 0:1], -c),
               reads=[tcb], writes=[b_thr])

    NOPP = cfg.get("no_prepass", False)
    if NOPP:
        pp1_step = pp1_flush = lambda *a: None
        tot1 = 0
    else:
        pp1_step, pp1_flush, tot1 = prepass_stream(d_w1p, NF * 128)

    def _cut(k):
        if MAXPH < k:
            sch.finalize()
            sch.emit()
            return True
        return False

    sch.set_barrier()
    p0 = Arena(nc, PP_END, SB_LIMIT, "p0")
    xt = [p0.alloc(f"xt{i}", [128, D], F32) for i in (0, 1)]
    xnt = p0.alloc("xnt", [128, D], F32)
    stg_h = [p0.alloc(f"xstg{p}", [128, ND * SC], BF16) for p in (0, 1)]
    stg = stg_h
    b_xt = [sch.buf(), sch.buf()]
    b_xnt = sch.buf()
    b_stg = [sch.buf(), sch.buf()]

    BNW = min(512, D)
    nbn = D // BNW
    for t in range(NTS):
        sl = t % 2
        sch.op("sync", lambda e, t=t, sl=sl: e.dma_start(
            xt[sl][:, :], d_x[t128(t), :]),
            writes=[b_xt[sl]], dma=True)
        if "L" not in P0DBG:
            sch.op("vector", lambda e, sl=sl: e.tensor_copy(
                xnt[:, :], xt[sl][:, :]), reads=[b_xt[sl]], writes=[b_xnt])
        for c in (range(nbn) if "L" in P0DBG else ()):
            sch.op("vector", lambda e, c=c, sl=sl: e.bn_stats(
                st6[:, 6 * c:6 * (c + 1)], xt[sl][:, BNW * c:BNW * (c + 1)]),
                reads=[b_xt[sl]], writes=[b_st6])
        if "L" not in P0DBG:
            pass
        else:
         sch.op("vector", lambda e: e.bn_aggr(mv[:, 0:2], st6[:, 0:6 * nbn]),
                reads=[b_st6], writes=[b_mv])
        if "L" in P0DBG:
         sch.op("scalar", lambda e: e.activation(
             sv[:, 6:7], mv[:, 1:2], AF.Sqrt, bias=cst[:, 0:1], scale=1.0),
             reads=[b_mv, b_cst], writes=[b_sv])
         sch.op("vector", lambda e: e.reciprocal(sv[:, 0:1], sv[:, 6:7]),
                reads=[b_sv], writes=[b_sv])
         sch.op("vector", lambda e: e.tensor_scalar_mul(
             sv[:, 1:2], mv[:, 0:1], -1.0), reads=[b_mv], writes=[b_sv])
         sch.op("vector", lambda e, sl=sl: e.tensor_scalar(
             xnt[:, :], xt[sl][:, :], sv[:, 1:2], sv[:, 0:1],
             ALU.add, ALU.mult),
             reads=[b_xt[sl], b_sv], writes=[b_xnt])
        ch, tl = t // (SC // 128), t % (SC // 128)
        for j0 in (range(0, ND, 4) if "T" in P0DBG else ()):
            ng = min(4, ND - j0)
            kk = trc[0]
            trc[0] += 1
            bb = kk % 2
            for jj in range(ng):
                sch.op("tensor", lambda e, j=j0 + jj, jj=jj, bb=bb:
                       e.transpose(
                           ps_tr[:, bb * 512 + jj * 128:
                                 bb * 512 + jj * 128 + 128],
                           xnt[:, t128(j)], ident[:, :]),
                       reads=[b_xnt, b_ident],
                       writes=[b_tr[bb]] if jj == 0 else [],
                       writes_nodep=[] if jj == 0 else [b_tr[bb]])
            if "S" in P0DBG:
                srcv = ps_tr[:, bb * 512:bb * 512 + ng * 128].rearrange(
                    "p (g c) -> p g c", c=128)
                off = j0 * SC + tl * 128
                dims = [[ND * SC, 128], [SC, ng], [1, 128]]
                dsth = bass.AP(stg_h[0], off, dims)
                dstl = bass.AP(stg_h[1], off, dims)
                sch.op("scalar", lambda e, srcv=srcv, dsth=dsth:
                       e.copy(dsth, srcv),
                       reads=[b_tr[bb]], writes=[b_stg[0]])
                sch.op("vector", lambda e, srcv=srcv, dsth=dsth, dstl=dstl:
                       e.scalar_tensor_tensor(dstl, srcv, 1.0, dsth,
                                              ALU.mult, ALU.subtract),
                       reads=[b_tr[bb], b_stg[0]], writes=[b_stg[1]])
        if tl == SC // 128 - 1 and "W" in P0DBG:
            for p in range(2):
                sch.op("scalar", lambda e, p=p, ch=ch: e.dma_start(
                    d_xn[p][t128(ch), :], stg[p][:, :]),
                    reads=[b_stg[p]], writes=[b_xn_dram[p][ch]], dma=True)
        pp1_step(128 // NTS + 1)

    if not NOPP:
        pp1_flush()
        prepass_fin(tot1, 0, DFF * D)
    else:
        sch.op("vector", lambda e: e.memset(thr[:, 0:4], 0.1),
               writes=[b_thr])
    if _cut(2):
        return nc

    sch.set_barrier()
    f1 = Arena(nc, PP_END, SB_LIMIT, "f1")
    xnf = [f1.alloc(f"xnf{p}", [128, NCH * ND * SC], BF16) for p in (0, 1)]
    w1s = [f1.alloc(f"w1s{i}", [128, ND * 128], F32) for i in (0, 1)]
    w1m = [f1.alloc(f"w1m{i}", [128, ND * 128], BF16) for i in (0, 1)]
    w1q = [f1.alloc(f"w1q{i}", [128, ND * 128], BF16) for i in (0, 1)]
    gt = [f1.alloc(f"gt{i}", [128, SC], F32) for i in (0, 1)]
    gst = [[f1.alloc(f"gst{i}{p}", [128, SC], BF16) for p in (0, 1)]
           for i in (0, 1)]
    b_xnf = [sch.buf(), sch.buf()]
    b_w1s = [sch.buf(), sch.buf()]
    b_w1m = [sch.buf(), sch.buf()]
    b_w1q = [sch.buf(), sch.buf()]
    b_gt = [sch.buf(), sch.buf()]
    b_gst = [[sch.buf(), sch.buf()], [sch.buf(), sch.buf()]]

    for p in range(2):
        for ch in range(NCH):
            sch.op("sync", lambda e, p=p, ch=ch: e.dma_start(
                xnf[p][:, ch * ND * SC:(ch + 1) * ND * SC],
                d_xn[p][t128(ch), :]),
                reads=[b_xn_dram[p][ch]], writes=[b_xnf[p]], dma=True)

    if NOPP:
        pp2_step = pp2_flush = lambda *a: None
        tot2 = 0
    else:
        pp2_step, pp2_flush, tot2 = prepass_stream(d_w2t, DFF)

    def w1_fetch_quant(ft):
        sl = ft % 2
        sch.op("sync", lambda e, ft=ft, sl=sl: e.dma_start(
            w1s[sl][:, :], d_w1p[t128(ft), :]),
            writes=[b_w1s[sl]], dma=True)
        sch.op("vector", lambda e, sl=sl: e.tensor_single_scalar(
            w1m[sl][:, :], w1s[sl][:, :], thr[:, 1:2], ALU.is_le),
            reads=[b_w1s[sl], b_thr], writes=[b_w1m[sl]])
        sch.op("vector", lambda e, sl=sl: e.scalar_tensor_tensor(
            w1q[sl][:, :], w1s[sl][:, :], thr[:, 0:1], w1m[sl][:, :],
            ALU.is_ge, ALU.subtract),
            reads=[b_w1s[sl], b_w1m[sl], b_thr], writes=[b_w1q[sl]])

    w1_fetch_quant(0)
    for ft in range(NF):
        sl = ft % 2
        if ft + 1 < NF:
            w1_fetch_quant(ft + 1)
        pp2_step(128 // NF + 2 if NF >= 32 else 64)
        for ch in range(NCH):
            pm = (ft * NCH + ch) % 2
            first = True
            for j in range(ND):
                for p in range(2):
                    base = ch * ND * SC + j * SC
                    sch.op("tensor", lambda e, j=j, p=p, pm=pm, base=base,
                           sl=sl, first=first, last=(j == ND - 1 and p == 1):
                           e.matmul(
                               ps_m[pm][:, 0:SC], w1q[sl][:, t128(j)],
                               xnf[p][:, base:base + SC],
                               start=first, stop=last),
                           reads=[b_w1q[sl], b_xnf[p]],
                           writes=[b_m[pm]] if first else [],
                           writes_nodep=[] if first else [b_m[pm]])
                    first = False
            gsl = (ft * NCH + ch) % 2
            sch.op("scalar", lambda e, ft=ft, pm=pm, gsl=gsl: e.activation(
                gt[gsl][:, :], ps_m[pm][:, 0:SC], AF.Gelu,
                bias=b1c[:, ft:ft + 1], scale=1.0),
                reads=[b_m[pm], b_b1c], writes=[b_gt[gsl]])
            sch.op("scalar", lambda e, ft=ft, pm=pm, gsl=gsl:
                   e.activation(gst[gsl][0][:, :], ps_m[pm][:, 0:SC],
                                AF.Gelu, bias=b1c[:, ft:ft + 1], scale=1.0),
                   reads=[b_m[pm], b_b1c], writes=[b_gst[gsl][0]])
            sch.op("vector", lambda e, gsl=gsl: e.scalar_tensor_tensor(
                gst[gsl][1][:, :], gt[gsl][:, :], 1.0, gst[gsl][0][:, :],
                ALU.mult, ALU.subtract),
                reads=[b_gt[gsl], b_gst[gsl][0]], writes=[b_gst[gsl][1]])
            for p in range(2):
                sch.op("scalar", lambda e, p=p, ft=ft, ch=ch, gsl=gsl:
                       e.dma_start(d_g[p][t128(ft), ch * SC:(ch + 1) * SC],
                                   gst[gsl][p][:, :]),
                       reads=[b_gst[gsl][p]], writes=[b_g_dram[p]], dma=True)

    if _cut(3):
        return nc


    if not NOPP:
        pp2_flush()
        prepass_fin(tot2, 1, DFF * D)

    sch.set_barrier()
    aa = Arena(nc, PERS_END, SB_LIMIT, "att")
    xna = [aa.alloc(f"xna{p}", [128, ND * SC], BF16) for p in (0, 1)]
    WCOLS = 3 * ND * 128
    PW = min(256, WCOLS)
    wqs = [aa.alloc(f"wqs{i}", [128, PW], F32) for i in (0, 1)]
    wsl = [[aa.alloc(f"wsl{par}_{p}", [128, G * WCOLS], BF16)
            for p in (0, 1)] for par in (0, 1)]
    qh = [[aa.alloc(f"q{pq}_{p}", [128, S], BF16) for p in (0, 1)]
          for pq in range(2)]
    kh = [[aa.alloc(f"k{pq}_{p}", [128, S], BF16) for p in (0, 1)]
          for pq in range(2)]
    vh = [[aa.alloc(f"v{pq}_{p}", [128, S], BF16) for p in (0, 1)]
          for pq in range(2)]
    vf = [aa.alloc("vf0", [128, SC], F32)]
    af32 = [aa.alloc(f"af32_{i}", [128, SBW], F32) for i in range(3)]
    ath_h = [aa.alloc(f"at_{p}", [128, NTS * QC], BF16) for p in (0, 1)]
    ath = ath_h
    yst = [[aa.alloc(f"yst{i}{p}", [128, QC], BF16) for p in (0, 1)]
           for i in (0,)]
    asv = aa.alloc("asv", [128, 32], F32)
    b_xna = [sch.buf(), sch.buf()]
    b_wqs = [sch.buf(), sch.buf()]
    b_wsl = [[sch.buf(), sch.buf()] for _ in (0, 1)]
    b_qh = [[sch.buf(), sch.buf()] for _ in range(2)]
    b_kh = [[sch.buf(), sch.buf()] for _ in range(2)]
    b_vh = [[sch.buf(), sch.buf()] for _ in range(2)]
    pmg = [0]
    b_vf = [sch.buf()]
    b_af32 = [sch.buf() for _ in range(3)]
    b_ath = [sch.buf(), sch.buf()]
    b_yst = [[sch.buf(), sch.buf()]]
    b_asv = [sch.buf(), sch.buf()]

    def fill_wgroup(grp):
        par = grp % 2
        pieces = [(o, min(PW, WCOLS - o)) for o in range(0, WCOLS, PW)]
        for h2 in range(G):
            head = grp * G + h2
            for pc, (po, pw_) in enumerate(pieces):
                wsl2 = (h2 * len(pieces) + pc) % 2
                sch.op("sync", lambda e, head=head, po=po, pw_=pw_, wsl2=wsl2:
                       e.dma_start(wqs[wsl2][:, 0:pw_],
                                   d_wqkv[t128(head), po:po + pw_]),
                       writes=[b_wqs[wsl2]], dma=True)
                co = h2 * WCOLS + po
                sch.op("gpsimd", lambda e, co=co, pw_=pw_, wsl2=wsl2, par=par:
                       e.tensor_copy(wsl[par][0][:, co:co + pw_],
                                     wqs[wsl2][:, 0:pw_]),
                       reads=[b_wqs[wsl2]], writes=[b_wsl[par][0]])
                sch.op("gpsimd", lambda e, co=co, pw_=pw_, wsl2=wsl2, par=par:
                       e.tensor_tensor(
                           wsl[par][1][:, co:co + pw_], wqs[wsl2][:, 0:pw_],
                           wsl[par][0][:, co:co + pw_], ALU.subtract),
                       reads=[b_wqs[wsl2], b_wsl[par][0]],
                       writes=[b_wsl[par][1]])

    fill_wgroup(0)
    for grp in range(H // G):

        par = grp % 2

        def mk_proj_tasks(g2):
            pq = g2 % 2
            tasks = []

            def wap(p, h2, tgt, j, pq=pq):
                co = h2 * WCOLS + (tgt * ND + j) * 128
                return wsl[pq][p][:, co:co + 128]

            for ch in range(NCH):
                def t_load(ch=ch):
                    for p in range(2):
                        sch.op("sync", lambda e, p=p, ch=ch: e.dma_start(
                            xna[p][:, :], d_xn[p][t128(ch), :]),
                            reads=[b_xn_dram[p][ch]], writes=[b_xna[p]],
                            dma=True)
                tasks.append(t_load)
                for tgt in range(3):
                    def t_grp(ch=ch, tgt=tgt, pq=pq, wap=wap):
                        pm = pmg[0] % 2
                        pmg[0] += 1
                        h2 = 0
                        first = True
                        for j in range(ND):
                            for (pw, px) in ((0, 0), (0, 1), (1, 0)):
                                last = (j == ND - 1 and (pw, px) == (1, 0))
                                sch.op("tensor",
                                       lambda e, pw=pw, px=px, tgt=tgt,
                                       j=j, pm=pm, first=first, last=last,
                                       wap=wap: e.matmul(
                                           ps_m[pm][:, 0:SC],
                                           wap(pw, 0, tgt, j),
                                           xna[px][:, j * SC:j * SC + SC],
                                           start=first, stop=last),
                                       reads=[b_wsl[pq][pw], b_xna[px]],
                                       writes=[b_m[pm]] if first else [],
                                       writes_nodep=[] if first
                                       else [b_m[pm]])
                                first = False
                        if tgt < 2:
                            dst = qh[pq] if tgt == 0 else kh[pq]
                            dbuf = b_qh[pq] if tgt == 0 else b_kh[pq]
                            split_psum(ps_m[pm][:, 0:SC],
                                       dst[0][:, ch * SC:(ch + 1) * SC],
                                       dst[1][:, ch * SC:(ch + 1) * SC],
                                       b_m[pm], dbuf[0], dbuf[1])
                        else:
                            vsl = 0
                            sch.op("vector", lambda e, pm=pm, vsl=vsl:
                                   e.tensor_copy(vf[vsl][:, :],
                                                 ps_m[pm][:, 0:SC]),
                                   reads=[b_m[pm]], writes=[b_vf[vsl]])
                            for tv0 in range(0, SC // 128, 4):
                                ngv = min(4, SC // 128 - tv0)
                                kk = trc[0]
                                trc[0] += 1
                                bb = kk % 2
                                for jj in range(ngv):
                                    sch.op("tensor",
                                           lambda e, tv=tv0 + jj, jj=jj,
                                           bb=bb, vsl=vsl: e.transpose(
                                               ps_tr[:, bb * 512 + jj * 128:
                                                     bb * 512 + jj * 128
                                                     + 128],
                                               vf[vsl][:, t128(tv)],
                                               ident[:, :]),
                                           reads=[b_vf[vsl], b_ident],
                                           writes=[b_tr[bb]] if jj == 0
                                           else [],
                                           writes_nodep=[] if jj == 0
                                           else [b_tr[bb]])
                                co = (ch * (SC // 128) + tv0) * 128
                                wv = ngv * 128
                                split_psum(ps_tr[:, bb * 512:bb * 512 + wv],
                                           vh[pq][0][:, co:co + wv],
                                           vh[pq][1][:, co:co + wv],
                                           b_tr[bb], b_vh[pq][0],
                                           b_vh[pq][1])
                    tasks.append(t_grp)
            return tasks

        if grp == 0:
            for t in mk_proj_tasks(0):
                t()
        if grp + 1 < H // G:
            fill_wgroup(grp + 1)
        feeder = mk_proj_tasks(grp + 1) if grp + 1 < H // G else []

        def feed():
            if feeder:
                feeder.pop(0)()

        for h2 in range(G):
            head = grp * G + h2
            for til in range(QT):
                for r in range(til + 1, QT):
                    for p in range(2):
                        off = r * QC + til * 128
                        dims = [[NTS * QC, 128], [QT * QC, NTS // QT],
                                [1, 128]]
                        ap = bass.AP(ath_h[p], off, dims)
                        sch.op("vector", lambda e, ap=ap: e.memset(ap, 0.0),
                               writes=[b_ath[p]])
            pending_tr = []
            pending_av = [None]

            def flush_tr():
                for emit in pending_tr:
                    emit()
                pending_tr.clear()

            for qc in range(NQC):
                KT = (qc + 1) * QT
                for ti in range(QT):
                    ig = qc * QT + ti
                    klen = (ig + 1) * 128
                    nm = (klen + 511) // 512
                    p2 = ti % 2
                    B = 16 * p2
                    sb = b_asv[p2]
                    asl = ig % 3
                    for m in range(nm):
                        w = min(512, klen - m * 512)
                        slot = sb_cnt[0] % 4
                        sb_cnt[0] += 1
                        for (pq, pk) in ((0, 0), (0, 1), (1, 0)):
                            first = (pq, pk) == (0, 0)
                            sch.op("tensor",
                                   lambda e, pq=pq, pk=pk, par=par,
                                   ig=ig, m=m,
                                   slot=slot, w=w, first=first: e.matmul(
                                       ps_big[:, slot * 512:slot * 512 + w],
                                       qh[par][pq][:, t128(ig)],
                                       kh[par][pk][:, m * 512:m * 512 + w],
                                       start=first,
                                       stop=(pq, pk) == (1, 0)),
                                   reads=[b_qh[par][pq], b_kh[par][pk]],
                                   writes=[b_sb4[slot]] if first else [],
                                   writes_nodep=[] if first else
                                   [b_sb4[slot]])
                        sch.op("vector", lambda e, m=m, w=w, slot=slot,
                               asl=asl: e.tensor_copy(
                                   af32[asl][:, m * 512:m * 512 + w],
                                   ps_big[:, slot * 512:slot * 512 + w]),
                               reads=[b_sb4[slot]], writes=[b_af32[asl]])
                        if m < nm - 1:
                            sch.op("vector", lambda e, m=m, w=w, B=B, asl=asl:
                                   e.reduce_max(
                                       asv[:, B + m:B + m + 1],
                                       af32[asl][:, m * 512:m * 512 + w],
                                       AX.X),
                                   reads=[b_af32[asl]], writes=[sb])
                    sch.op("vector", lambda e, ig=ig, klen=klen, asl=asl:
                           e.tensor_tensor(af32[asl][:, ig * 128:klen],
                                           af32[asl][:, ig * 128:klen],
                                           mask[:, :], ALU.add),
                           reads=[b_af32[asl], b_mask], writes=[b_af32[asl]])
                    m = nm - 1
                    w = klen - m * 512
                    sch.op("vector", lambda e, m=m, w=w, B=B, asl=asl:
                           e.reduce_max(asv[:, B + m:B + m + 1],
                                        af32[asl][:, m * 512:m * 512 + w],
                                        AX.X),
                           reads=[b_af32[asl]], writes=[sb])
                    sch.op("vector", lambda e, nm=nm, B=B: e.reduce_max(
                        asv[:, B + 4:B + 5], asv[:, B:B + nm], AX.X),
                        reads=[sb], writes=[sb])
                    sch.op("vector", lambda e, B=B: e.tensor_scalar_mul(
                        asv[:, B + 5:B + 6], asv[:, B + 4:B + 5], -scale),
                        reads=[sb], writes=[sb])
                    sch.op("scalar", lambda e, klen=klen, asl=asl, B=B:
                           e.activation(af32[asl][:, 0:klen],
                                        af32[asl][:, 0:klen], AF.Exp,
                                        bias=asv[:, B + 5:B + 6],
                                        scale=cst[:, 1:2],
                                        accum_out=asv[:, B + 6:B + 7]),
                           reads=[sb, b_cst],
                           writes=[b_af32[asl], sb])
                    sch.op("vector", lambda e, B=B: e.reciprocal(
                        asv[:, B + 7:B + 8], asv[:, B + 6:B + 7]),
                        reads=[sb], writes=[sb])
                    sch.op("vector", lambda e, klen=klen, asl=asl, B=B:
                           e.tensor_scalar_mul(af32[asl][:, 0:klen],
                                               af32[asl][:, 0:klen],
                                               asv[:, B + 7:B + 8]),
                           reads=[sb], writes=[b_af32[asl]])

                    def mk_tr(ig=ig, ti=ti, asl=asl):
                        def emit():
                            for j0 in range(0, ig + 1, 4):
                                ng = min(4, ig + 1 - j0)
                                k = trc[0]
                                trc[0] += 1
                                b = k % 2
                                for jj in range(ng):
                                    sch.op("tensor",
                                           lambda e, j=j0 + jj, jj=jj, b=b,
                                           asl=asl: e.transpose(
                                               ps_tr[:, b * 512 + jj * 128:
                                                     b * 512 + jj * 128 + 128],
                                               af32[asl][:, t128(j)],
                                               ident[:, :]),
                                           reads=[b_af32[asl], b_ident],
                                           writes=[b_tr[b]] if jj == 0 else [],
                                           writes_nodep=[] if jj == 0
                                           else [b_tr[b]])
                                srcv = ps_tr[:, b * 512:b * 512 + ng * 128
                                             ].rearrange("p (g c) -> p g c",
                                                         c=128)
                                off = j0 * QC + ti * 128
                                dims = [[NTS * QC, 128], [QC, ng], [1, 128]]
                                dsth = bass.AP(ath_h[0], off, dims)
                                dstl = bass.AP(ath_h[1], off, dims)
                                sch.op("scalar",
                                       lambda e, srcv=srcv, dsth=dsth:
                                       e.copy(dsth, srcv),
                                       reads=[b_tr[b]], writes=[b_ath[0]])
                                sch.op("vector",
                                       lambda e, srcv=srcv, dsth=dsth,
                                       dstl=dstl: e.scalar_tensor_tensor(
                                           dstl, srcv, 1.0, dsth,
                                           ALU.mult, ALU.subtract),
                                       reads=[b_tr[b], b_ath[0]],
                                       writes=[b_ath[1]])
                        return emit

                    if ti == 0 and pending_av[0] is not None:
                        flush_tr()
                        pending_av[0]()
                        pending_av[0] = None
                    pending_tr.append(mk_tr())
                    if len(pending_tr) > 2:
                        pending_tr.pop(0)()
                    feed()

                def mk_av(qc=qc, KT=KT, par=par, head=head):
                    def emit():
                        pm = pmg[0] % 2
                        pmg[0] += 1
                        first = True
                        for j in range(KT):
                            for (pv, pa) in ((0, 0), (0, 1), (1, 0)):
                                last = (j == KT - 1 and (pv, pa) == (1, 0))
                                sch.op("tensor",
                                       lambda e, pv=pv, pa=pa, par=par,
                                       j=j, pm=pm, first=first, last=last:
                                       e.matmul(
                                           ps_m[pm][:, 0:QC],
                                           vh[par][pv][:, t128(j)],
                                           ath[pa][:, j * QC:(j + 1) * QC],
                                           start=first, stop=last),
                                       reads=[b_vh[par][pv], b_ath[pa]],
                                       writes=[b_m[pm]] if first else [],
                                       writes_nodep=[] if first
                                       else [b_m[pm]])
                                first = False
                        ysl = 0
                        split_psum_v(ps_m[pm][:, 0:QC], yst[ysl][0][:, :],
                                     yst[ysl][1][:, :], b_m[pm],
                                     b_yst[ysl][0], b_yst[ysl][1])
                        for p in range(2):
                            sch.op("scalar",
                                   lambda e, p=p, head=head, qc=qc, ysl=ysl:
                                   e.dma_start(d_y[p][t128(head),
                                                      qc * QC:(qc + 1) * QC],
                                               yst[ysl][p][:, :]),
                                   reads=[b_yst[ysl][p]],
                                   writes=[b_y_dram[p]], dma=True)
                    return emit

                pending_av[0] = mk_av()
            flush_tr()
            if pending_av[0] is not None:
                pending_av[0]()
                pending_av[0] = None
            while feeder:
                feeder.pop(0)()

    if _cut(4):
        return nc

    sch.set_barrier()
    acc = nc.alloc_sbuf_tensor_at("acc", [128, NTS * D], F32, offset=ACC_OFF)
    op = Arena(nc, PERS_END, ACC_OFF, "op")
    SG = min(2, NTS)
    ysl_ = [[op.alloc(f"ysl{j}_{p}", [128, SG * 128], BF16) for p in (0, 1)]
            for j in range(ND)]
    wos = [op.alloc(f"wos{i}", [128, EC], F32) for i in (0, 1)]
    wol = [[op.alloc(f"wol{j}_{p}", [128, EC], BF16) for p in (0, 1)]
           for j in range(ND)]
    xot = [op.alloc(f"xot{i}", [128, EC], F32) for i in (0, 1)]
    b_acc = [sch.buf() for _ in range(NTS)]
    b_ysl = [[sch.buf(), sch.buf()] for _ in range(ND)]
    b_wos = [sch.buf(), sch.buf()]
    b_wol = [[sch.buf(), sch.buf()] for _ in range(ND)]
    b_xot = [sch.buf(), sch.buf()]

    for ec in range(NEC):
        for j in range(ND):
            wsl2 = j % 2
            sch.op("sync", lambda e, j=j, ec=ec, wsl2=wsl2: e.dma_start(
                wos[wsl2][:, :], d_wout[t128(j), ec * EC:(ec + 1) * EC]),
                writes=[b_wos[wsl2]], dma=True)
            sch.op("scalar", lambda e, j=j, wsl2=wsl2: e.copy(
                wol[j][0][:, :], wos[wsl2][:, :]),
                reads=[b_wos[wsl2]], writes=[b_wol[j][0]])
            sch.op("vector", lambda e, j=j, wsl2=wsl2:
                   e.scalar_tensor_tensor(
                       wol[j][1][:, :], wos[wsl2][:, :], 1.0,
                       wol[j][0][:, :], ALU.mult, ALU.subtract),
                   reads=[b_wos[wsl2], b_wol[j][0]], writes=[b_wol[j][1]])
        for sg in range(NTS // SG):
            for j in range(ND):
                for p in range(2):
                    sch.op("sync", lambda e, j=j, p=p, sg=sg: e.dma_start(
                        ysl_[j][p][:, :],
                        d_y[p][t128(j), sg * SG * 128:(sg + 1) * SG * 128]),
                        reads=[b_y_dram[p]], writes=[b_ysl[j][p]], dma=True)
            for st2 in range(SG):
                st = sg * SG + st2
                pm = (ec * NTS + sg * SG + st2) % 2
                first = True
                for j in range(ND):
                    for (py, pw) in ((0, 0), (0, 1), (1, 0)):
                        last = (j == ND - 1 and (py, pw) == (1, 0))
                        sch.op("tensor",
                               lambda e, py=py, pw=pw, j=j, st2=st2, pm=pm,
                               first=first, last=last: e.matmul(
                                   ps_m[pm][:, 0:EC],
                                   ysl_[j][py][:, t128(st2)],
                                   wol[j][pw][:, :],
                                   start=first, stop=last),
                               reads=[b_ysl[j][py], b_wol[j][pw]],
                               writes=[b_m[pm]] if first else [],
                               writes_nodep=[] if first else [b_m[pm]])
                        first = False
                xsl = st2 % 2
                sch.op("scalar", lambda e, st=st, ec=ec, xsl=xsl: e.dma_start(
                    xot[xsl][:, :], d_x[t128(st), ec * EC:(ec + 1) * EC]),
                    reads=[b_xot[xsl]], writes=[b_xot[xsl]], dma=True)
                sch.op("vector", lambda e, st=st, ec=ec, pm=pm, xsl=xsl:
                       e.scalar_tensor_tensor(
                           acc[:, st * D + ec * EC:st * D + (ec + 1) * EC],
                           ps_m[pm][:, 0:EC], 1.0, xot[xsl][:, :],
                           ALU.mult, ALU.add),
                       reads=[b_m[pm], b_xot[xsl]], writes=[b_acc[st]])

    if _cut(5):
        return nc
    sch.set_barrier()
    f2 = Arena(nc, PERS_END, ACC_OFF, "f2")
    FCn = NF // FC
    gsl_ = [[f2.alloc(f"gsl{f}_{p}", [128, S], BF16) for p in (0, 1)]
            for f in range(FC)]
    w2s = [f2.alloc("w2s0", [128, D], F32)]
    w2m = [f2.alloc("w2m0", [128, D], BF16)]
    w2q = [f2.alloc(f"w2q{f}", [128, D], BF16) for f in range(2 * FC)]
    b_gsl = [[[sch.buf(), sch.buf()] for _ in range(2)]
             for _ in range(FC)]
    b_w2s = [sch.buf()]
    b_w2m = [sch.buf()]
    b_w2q = [sch.buf() for _ in range(2 * FC)]

    def w2_fetch_quant(fc):
        cpar = fc % 2
        for f in range(FC):
            ft = fc * FC + f
            fq = cpar * FC + f
            sch.op("sync", lambda e, ft=ft: e.dma_start(
                w2s[0][:, :], d_w2t[t128(ft), :]),
                reads=[b_w2s[0]], writes=[b_w2s[0]], dma=True)
            sch.op("vector", lambda e: e.tensor_single_scalar(
                w2m[0][:, :], w2s[0][:, :], thr[:, 3:4], ALU.is_le),
                reads=[b_w2s[0], b_thr], writes=[b_w2m[0]])
            sch.op("vector", lambda e, fq=fq:
                   e.scalar_tensor_tensor(
                       w2q[fq][:, :], w2s[0][:, :], thr[:, 2:3],
                       w2m[0][:, :], ALU.is_ge, ALU.subtract),
                   reads=[b_w2s[0], b_w2m[0], b_thr],
                   writes=[b_w2q[fq]])

    HS = S // 2

    def g_load(fc, half):
        for f in range(FC):
            ft = fc * FC + f
            for p in range(2):
                sch.op("sync", lambda e, ft=ft, f=f, p=p, half=half:
                       e.dma_start(
                           gsl_[f][p][:, half * HS:(half + 1) * HS],
                           d_g[p][t128(ft), half * HS:(half + 1) * HS]),
                       reads=[b_g_dram[p], b_gsl[f][half][p]],
                       writes=[b_gsl[f][half][p]], dma=True)

    w2_fetch_quant(0)
    g_load(0, 0)
    g_load(0, 1)
    for fc in range(FCn):
        cpar = fc % 2
        if fc + 1 < FCn:
            w2_fetch_quant(fc + 1)
        for st in range(NTS):
            half = (st * 128) // HS
            if st == NTS // 2 and fc + 1 < FCn:
                g_load(fc + 1, 0)
            for ec in range(NEC):
                pm = (st * NEC + ec) % 2
                first = True
                for f in range(FC):
                    fq = cpar * FC + f
                    for p in range(2):
                        last = (f == FC - 1 and p == 1)
                        sch.op("tensor",
                               lambda e, f=f, fq=fq, p=p, st=st, ec=ec,
                               pm=pm, first=first, last=last: e.matmul(
                                   ps_m[pm][:, 0:EC],
                                   gsl_[f][p][:, t128(st)],
                                   w2q[fq][:, ec * EC:(ec + 1) * EC],
                                   start=first, stop=last),
                               reads=[b_gsl[f][half][p], b_w2q[fq]],
                               writes=[b_m[pm]] if first else [],
                               writes_nodep=[] if first else [b_m[pm]])
                        first = False
                sch.op("vector", lambda e, st=st, ec=ec, pm=pm:
                       e.scalar_tensor_tensor(
                           acc[:, st * D + ec * EC:st * D + (ec + 1) * EC],
                           ps_m[pm][:, 0:EC], 1.0,
                           acc[:, st * D + ec * EC:st * D + (ec + 1) * EC],
                           ALU.mult, ALU.add),
                       reads=[b_m[pm]], writes=[b_acc[st]])
        if fc + 1 < FCn:
            g_load(fc + 1, 1)

    if _cut(6):
        return nc
    sch.set_barrier()
    ea = Arena(nc, PERS_END, ACC_OFF, "ea")
    b2row = ea.alloc("b2row", [1, D], F32)
    b2b = ea.alloc("b2b", [128, D], F32)
    ost = [ea.alloc(f"ost{i}", [128, EC], F32) for i in (0, 1)]
    b_b2row, b_b2b = sch.buf(), sch.buf()
    b_ost = [sch.buf(), sch.buf()]

    sch.op("sync", lambda e: e.dma_start(b2row[0:1, :], d_b2r),
           writes=[b_b2row], dma=True)
    for ec in range(NEC):
        pm = ec % 2
        sch.op("tensor", lambda e, ec=ec, pm=pm: e.matmul(
            ps_m[pm][:, 0:EC], ones[0:1, 0:128],
            b2row[0:1, ec * EC:(ec + 1) * EC], start=True, stop=True),
            reads=[b_b2row, b_ones], writes=[b_m[pm]])
        sch.op("vector", lambda e, ec=ec, pm=pm: e.tensor_copy(
            b2b[:, ec * EC:(ec + 1) * EC], ps_m[pm][:, 0:EC]),
            reads=[b_m[pm]], writes=[b_b2b])
    for st in range(NTS):
        for ec in range(NEC):
            osl = (st * NEC + ec) % 2
            sch.op("vector", lambda e, st=st, ec=ec, osl=osl:
                   e.scalar_tensor_tensor(
                       ost[osl][:, :],
                       acc[:, st * D + ec * EC:st * D + (ec + 1) * EC],
                       1.0, b2b[:, ec * EC:(ec + 1) * EC],
                       ALU.mult, ALU.add),
                   reads=[b_acc[st], b_b2b, b_ost[osl]], writes=[b_ost[osl]])
            sch.op("scalar", lambda e, st=st, ec=ec, osl=osl: e.dma_start(
                d_out[t128(st), ec * EC:(ec + 1) * EC], ost[osl][:, :]),
                reads=[b_ost[osl]], writes=[b_out_dram], dma=True)

    sch.finalize()
    sch.emit()
    return nc


def prep_weights(cfg, w_qkv, w_out, w1, b1, w2, b2):
    D, H, DFF = cfg["D"], cfg["H"], cfg["DFF"]
    ND, NF = D // 128, DFF // 128
    w4 = np.asarray(w_qkv, np.float32).reshape(3, H, 128, ND, 128)
    wqkv = np.ascontiguousarray(np.transpose(w4, (1, 4, 0, 3, 2))
                                ).reshape(H * 128, 3 * ND * 128)
    woutT = np.ascontiguousarray(np.asarray(w_out, np.float32).T)
    w1r = np.asarray(w1, np.float32).reshape(NF, 128, ND, 128)
    w1p = np.ascontiguousarray(np.transpose(w1r, (0, 3, 2, 1))
                               ).reshape(NF * 128, D)
    w2t = np.ascontiguousarray(np.asarray(w2, np.float32).T)
    b1c = np.ascontiguousarray(np.asarray(b1, np.float32).reshape(NF, 128).T)
    b2r = np.ascontiguousarray(np.asarray(b2, np.float32).reshape(1, D))
    identc = np.eye(128, dtype=np.float32)
    idx = np.arange(128)
    maskneg = np.where(idx[None, :] <= idx[:, None], 0.0, NEG_BIG
                       ).astype(np.float32)
    return dict(wqkv=wqkv, woutT=woutT, w1p=w1p, w2t=w2t, b1c=b1c, b2r=b2r,
                identc=identc, maskneg=maskneg)


def kernel(L0_attn_x, L0_attn_w_qkv, L0_attn_w_out, L0_ffn_w1, L0_ffn_b1,
           L0_ffn_w2, L0_ffn_b2):
    cfg = default_cfg()
    n = cfg["n_cores"]
    x = np.ascontiguousarray(np.asarray(L0_attn_x, np.float32))
    wd = prep_weights(cfg, L0_attn_w_qkv, L0_attn_w_out, L0_ffn_w1,
                      L0_ffn_b1, L0_ffn_w2, L0_ffn_b2)
    nc = build_program(cfg)
    in_maps = [dict(x=x[c], **wd) for c in range(n)]
    res = run_bass_kernel_spmd(nc, in_maps, core_ids=list(range(n)))
    return np.stack([np.asarray(r["out"], np.float32)
                     for r in res.results], axis=0)


# version 30
# speedup vs baseline: 1.0110x; 1.0110x over previous
import numpy as np
import concourse.bass as bass
import concourse.mybir as mybir
from concourse.bass_utils import run_bass_kernel_spmd

F32 = mybir.dt.float32
BF16 = mybir.dt.bfloat16
AF = mybir.ActivationFunctionType
ALU = mybir.AluOpType
AX = mybir.AxisListType

NEG_BIG = -1.0e30
LN_EPS = 1.0e-5


def default_cfg():
    return dict(S=2048, D=2048, H=16, DFF=8192, G=1, SC=512, QC=512, EC=512,
                n_cores=8)


class Buf:
    __slots__ = ("w", "r", "dsem")

    def __init__(self, init=None):
        self.w = dict(init) if init else {}
        self.r = dict(init) if init else {}
        self.dsem = None


class Sched:
    COMPUTE_SEM = {"tensor": "T", "vector": "V", "scalar": "A", "gpsimd": "G"}
    QUEUE_SEM = {"sync": "qS", "scalar": "qA", "gpsimd": "qG"}
    ENGINES = ("tensor", "vector", "scalar", "gpsimd", "sync")

    def __init__(self, nc):
        self.nc = nc
        self.ops = {e: [] for e in self.ENGINES}
        self.sem = {}
        self.cnt = {}
        for s in ("T", "V", "A", "G", "qS", "qA", "qG"):
            self.sem[s] = nc.alloc_semaphore("sem_" + s)
            self.cnt[s] = 0

    def snap(self):
        return dict(self.cnt)

    def buf(self):
        return Buf(getattr(self, "_snap", None))

    def set_barrier(self):
        self._snap = dict(self.cnt)

    def op(self, engine, fn, reads=(), writes=(), writes_nodep=(),
           war=(), dma=False):
        need = {}
        for b in reads:
            for s, v in b.w.items():
                if need.get(s, 0) < v:
                    need[s] = v
        for b in war:
            for s, v in b.r.items():
                if need.get(s, 0) < v:
                    need[s] = v
        for b in writes:
            for d in (b.w, b.r):
                for s, v in d.items():
                    if need.get(s, 0) < v:
                        need[s] = v
        if dma:
            tb = (list(writes) + list(writes_nodep))[0]
            if tb.dsem is None:
                tb.dsem = f"qd{len(self.sem)}"
                self.sem[tb.dsem] = self.nc.alloc_semaphore(tb.dsem)
                self.cnt[tb.dsem] = 0
            s = tb.dsem
        else:
            s = self.COMPUTE_SEM[engine]
        amt = 16 if dma else 1
        self.cnt[s] += amt
        tok = (s, self.cnt[s])
        self.ops[engine].append((need, fn, s, amt))
        for b in list(writes) + list(writes_nodep):
            if b.w.get(s, 0) < tok[1]:
                b.w[s] = tok[1]
        for b in reads:
            if b.r.get(s, 0) < tok[1]:
                b.r[s] = tok[1]
        return tok

    def finalize(self):
        final = dict(self.cnt)
        self.ops["sync"].append((final, None, None, 0))

    def emit(self):
        nc = self.nc
        with nc.Block() as block:
            def runner(engine):
                oplist = self.ops[engine]
                sems = self.sem

                def go(eng):
                    wm = {}
                    for need, fn, s, amt in oplist:
                        for ns in sorted(need):
                            v = need[ns]
                            if wm.get(ns, 0) < v:
                                eng.wait_ge(sems[ns], v)
                                wm[ns] = v
                        if fn is not None:
                            inst = fn(eng)
                            inst.then_inc(sems[s], amt)
                return go

            block.tensor(runner("tensor"))
            block.vector(runner("vector"))
            block.scalar(runner("scalar"))
            block.gpsimd(runner("gpsimd"))
            block.sync(runner("sync"))


class Arena:

    def __init__(self, nc, base, limit, tag):
        self.nc = nc
        self.off = base
        self.limit = limit
        self.tag = tag
        self.n = 0

    def alloc(self, name, shape, dtype):
        nbytes = shape[1] * mybir.dt.size(dtype)
        nbytes = (nbytes + 31) & ~31
        assert self.off + nbytes <= self.limit, (
            f"{self.tag}/{name}: {self.off}+{nbytes} > {self.limit}")
        h = self.nc.alloc_sbuf_tensor_at(
            f"{self.tag}_{self.n}_{name}", shape, dtype, offset=self.off)
        self.n += 1
        self.off += nbytes
        return h


def t128(i):
    return slice(i * 128, (i + 1) * 128)


def build_program(cfg):
    S, D, H, DFF = cfg["S"], cfg["D"], cfg["H"], cfg["DFF"]
    G, SC, QC, EC = cfg["G"], cfg["SC"], cfg["QC"], cfg["EC"]
    HD = 128
    assert D == H * HD
    assert G == 1, "projection feeder assumes one head per group"
    NTS, ND, NF = S // 128, D // 128, DFF // 128
    NCH = S // SC
    QT = QC // 128
    NQC = S // QC
    NEC = D // EC
    FC = min(4, NF)
    scale = HD ** -0.5
    MAXPH = cfg.get("max_phase", 99)
    P0DBG = cfg.get("p0_dbg", "LTSW")

    nc = bass.Bass("TRN2", target_bir_lowering=False)

    d_x = nc.dram_tensor("x", [S, D], F32, kind="ExternalInput").ap()
    d_wqkv = nc.dram_tensor("wqkv", [H * 128, 3 * ND * 128], F32,
                            kind="ExternalInput").ap()
    d_wout = nc.dram_tensor("woutT", [D, D], F32, kind="ExternalInput").ap()
    d_w1p = nc.dram_tensor("w1p", [NF * 128, D], F32,
                           kind="ExternalInput").ap()
    d_w2t = nc.dram_tensor("w2t", [DFF, D], F32, kind="ExternalInput").ap()
    d_b1c = nc.dram_tensor("b1c", [128, NF], F32, kind="ExternalInput").ap()
    d_b2r = nc.dram_tensor("b2r", [1, D], F32, kind="ExternalInput").ap()
    d_ident = nc.dram_tensor("identc", [128, 128], F32,
                             kind="ExternalInput").ap()
    d_mask = nc.dram_tensor("maskneg", [128, 128], F32,
                            kind="ExternalInput").ap()
    d_out = nc.dram_tensor("out", [S, D], F32, kind="ExternalOutput").ap()
    d_xn = [nc.dram_tensor(f"xn_{p}", [NCH * 128, ND * SC], BF16).ap()
            for p in range(2)]
    d_y = [nc.dram_tensor(f"y_{p}", [D, S], BF16).ap() for p in range(2)]
    d_g = [nc.dram_tensor(f"g_{p}", [DFF, S], BF16).ap() for p in range(2)]

    SBW = min(S, 2048)
    ps_big = nc.alloc_psum_tensor("ps_big", [128, 2048], F32)
    ps_tr = nc.alloc_psum_tensor("ps_tr", [128, 1024], F32)
    ps_m = [nc.alloc_psum_tensor(f"ps_m{i}", [128, 512], F32) for i in (0, 1)]

    sch = Sched(nc)
    b_big = sch.buf()
    b_sb4 = [sch.buf() for _ in range(4)]
    sb_cnt = [0]
    b_tr = [sch.buf(), sch.buf()]
    b_m = [sch.buf(), sch.buf()]
    trc = [0]

    def tr_next():
        k = trc[0]
        trc[0] += 1
        b = k % 2
        return ps_tr[:, b * 512:b * 512 + 128], b_tr[b]

    SB_LIMIT = 229376 - 32
    ARENA_BASE = 17408
    ACC_BYTES = NTS * D * 4
    ACC_OFF = SB_LIMIT - ACC_BYTES
    pa = Arena(nc, ARENA_BASE, SB_LIMIT, "pers")
    ident = pa.alloc("ident", [128, 128], F32)
    mask = pa.alloc("mask", [128, 128], F32)
    ones = pa.alloc("ones", [1, 128], F32)
    b1c = pa.alloc("b1c", [128, max(NF, 8)], F32)
    thr = pa.alloc("thr", [128, 8], F32)
    sv = pa.alloc("sv", [128, 16], F32)
    cst = pa.alloc("cst", [128, 2], F32)
    st6 = pa.alloc("st6", [128, 6 * max((D + 511) // 512, 1)], F32)
    mv = pa.alloc("mv", [128, 2], F32)
    PERS_END = pa.off
    ppa = Arena(nc, PERS_END, SB_LIMIT, "pp")
    ppart = ppa.alloc("ppart",
                      [128, max(NF * D // 128, DFF * D // 16384, 128)], F32)
    pp2 = ppa.alloc("pp2", [128, 16], F32)
    PPW = min(512, D)
    ppw = [ppa.alloc(f"ppw{i}", [128, PPW], F32) for i in range(5)]
    PP_END = ppa.off
    assert PP_END < ACC_OFF

    b_ident, b_mask, b_ones, b_b1c = sch.buf(), sch.buf(), sch.buf(), sch.buf()
    b_thr = sch.buf()
    b_sv = sch.buf()
    b_cst = sch.buf()
    b_st6, b_mv = sch.buf(), sch.buf()
    b_ppart, b_pp2 = sch.buf(), sch.buf()
    b_ppw = [sch.buf() for _ in range(5)]
    b_ppwall = sch.buf()
    b_acc = None
    b_xn_dram = [[sch.buf() for _ in range(max(S // SC, 1))]
                 for _ in range(2)]
    b_y_dram = [sch.buf(), sch.buf()]
    b_g_dram = [sch.buf(), sch.buf()]
    b_out_dram = sch.buf()

    sch.op("sync", lambda e: e.dma_start(ident[:, :], d_ident),
           writes=[b_ident], dma=True)
    sch.op("sync", lambda e: e.dma_start(mask[:, :], d_mask),
           writes=[b_mask], dma=True)
    sch.op("sync", lambda e: e.dma_start(b1c[:, 0:NF], d_b1c),
           writes=[b_b1c], dma=True)
    sch.op("vector", lambda e: e.memset(ones[0:1, :], 1.0), writes=[b_ones])
    sch.op("vector", lambda e: e.memset(cst[:, 0:1], LN_EPS), writes=[b_cst])
    sch.op("vector", lambda e: e.memset(cst[:, 1:2], scale), writes=[b_cst])

    def split_psum(ps_ap, hi_ap, lo_ap, ps_buf, hi_buf, lo_buf):
        sch.op("scalar", lambda e, a=ps_ap, b=hi_ap: e.copy(b, a),
               reads=[ps_buf], writes=[hi_buf])
        sch.op("vector",
               lambda e, a=ps_ap, b=hi_ap, c=lo_ap:
               e.scalar_tensor_tensor(c, a, 1.0, b, ALU.mult, ALU.subtract),
               reads=[ps_buf, hi_buf], writes=[lo_buf])

    def split_psum_v(ps_ap, hi_ap, lo_ap, ps_buf, hi_buf, lo_buf):
        sch.op("vector", lambda e, a=ps_ap, b=hi_ap: e.tensor_copy(b, a),
               reads=[ps_buf], writes=[hi_buf])
        sch.op("vector",
               lambda e, a=ps_ap, b=hi_ap, c=lo_ap:
               e.scalar_tensor_tensor(c, a, 1.0, b, ALU.mult, ALU.subtract),
               reads=[ps_buf, hi_buf], writes=[lo_buf])

    def prepass_stream(d_w, rows):
        nblk = rows // 128
        npc = D // PPW
        pcw = PPW // 128
        chunks = [(r, q) for r in range(nblk) for q in range(npc)]
        state = {"i": 0}

        def step(k):
            i0 = state["i"]
            for idx in range(i0, min(i0 + k, len(chunks))):
                r, q = chunks[idx]
                i = idx % 5
                sch.op("gpsimd",
                       lambda e, r=r, q=q, i=i: e.dma_start(
                           ppw[i][:, :], d_w[t128(r), q * PPW:(q + 1) * PPW]),
                       war=[b_ppw[i]], writes_nodep=[b_ppw[i]],
                       dma=True)
                c0 = r * npc + q
                sch.op("scalar",
                       lambda e, c0=c0, i=i: e.activation(
                           ppw[i][:, :], ppw[i][:, :], AF.Abs,
                           accum_out=ppart[:, c0:c0 + 1]),
                       reads=[b_ppw[i]], writes=[b_ppw[i], b_ppart])
            state["i"] = min(i0 + k, len(chunks))

        def flush():
            step(len(chunks))
        return step, flush, nblk * npc

    def prepass_fin(tot, thr_col, nelem):
        k1 = 1
        while tot // k1 > 128:
            k1 *= 2
        sch.op("vector",
               lambda e, tot=tot, k1=k1: e.tensor_reduce(
                   pp2[:, 0:k1],
                   ppart[:, 0:tot].rearrange("p (k c) -> p k c", k=k1),
                   AX.X, ALU.add, False),
               reads=[b_ppart], writes=[b_pp2])
        sch.op("vector",
               lambda e, k1=k1, thr_col=thr_col: e.tensor_reduce(
                   pp2[:, 8 + thr_col:9 + thr_col],
                   pp2[:, 0:k1], AX.X, ALU.add, False),
               reads=[b_pp2], writes=[b_pp2])
        ta, tb = tr_next()
        sch.op("tensor",
               lambda e, ta=ta, thr_col=thr_col: e.transpose(
                   ta[0:1, :], pp2[:, 8 + thr_col:9 + thr_col], ident[:, :]),
               reads=[b_pp2, b_ident], writes=[tb])
        sch.op("vector",
               lambda e, ta=ta, thr_col=thr_col: e.tensor_reduce(
                   pp2[0:1, 12 + thr_col:13 + thr_col],
                   ta[0:1, :], AX.X, ALU.add, False),
               reads=[tb], writes=[b_pp2])
        tc, tcb = tr_next()
        sch.op("tensor",
               lambda e, tc=tc, thr_col=thr_col: e.matmul(
                   tc[:, 0:1],
                   ones[0:1, 0:128], pp2[0:1, 12 + thr_col:13 + thr_col],
                   start=True, stop=True),
               reads=[b_pp2, b_ones], writes=[tcb])
        c = 0.7 / float(nelem)
        sch.op("vector",
               lambda e, tc=tc, thr_col=thr_col, c=c: e.tensor_scalar_mul(
                   thr[:, 2 * thr_col:2 * thr_col + 1], tc[:, 0:1], c),
               reads=[tcb], writes=[b_thr])
        sch.op("vector",
               lambda e, tc=tc, thr_col=thr_col, c=c: e.tensor_scalar_mul(
                   thr[:, 2 * thr_col + 1:2 * thr_col + 2], tc[:, 0:1], -c),
               reads=[tcb], writes=[b_thr])

    NOPP = cfg.get("no_prepass", False)
    if NOPP:
        pp1_step = pp1_flush = lambda *a: None
        tot1 = 0
    else:
        pp1_step, pp1_flush, tot1 = prepass_stream(d_w1p, NF * 128)
        pp1_flush()

    def _cut(k):
        if MAXPH < k:
            sch.finalize()
            sch.emit()
            return True
        return False

    sch.set_barrier()
    p0 = Arena(nc, PP_END, SB_LIMIT, "p0")
    xt = [p0.alloc(f"xt{i}", [128, D], F32) for i in (0, 1)]
    xnt = p0.alloc("xnt", [128, D], F32)
    stg_h = [p0.alloc(f"xstg{p}", [128, ND * SC], BF16) for p in (0, 1)]
    stg = stg_h
    b_xt = [sch.buf(), sch.buf()]
    b_xnt = sch.buf()
    b_stg = [sch.buf(), sch.buf()]

    BNW = min(512, D)
    nbn = D // BNW
    for t in range(NTS):
        sl = t % 2
        sch.op("sync", lambda e, t=t, sl=sl: e.dma_start(
            xt[sl][:, :], d_x[t128(t), :]),
            writes=[b_xt[sl]], dma=True)
        if "L" not in P0DBG:
            sch.op("vector", lambda e, sl=sl: e.tensor_copy(
                xnt[:, :], xt[sl][:, :]), reads=[b_xt[sl]], writes=[b_xnt])
        for c in (range(nbn) if "L" in P0DBG else ()):
            sch.op("vector", lambda e, c=c, sl=sl: e.bn_stats(
                st6[:, 6 * c:6 * (c + 1)], xt[sl][:, BNW * c:BNW * (c + 1)]),
                reads=[b_xt[sl]], writes=[b_st6])
        if "L" not in P0DBG:
            pass
        else:
         sch.op("vector", lambda e: e.bn_aggr(mv[:, 0:2], st6[:, 0:6 * nbn]),
                reads=[b_st6], writes=[b_mv])
        if "L" in P0DBG:
         sch.op("scalar", lambda e: e.activation(
             sv[:, 6:7], mv[:, 1:2], AF.Sqrt, bias=cst[:, 0:1], scale=1.0),
             reads=[b_mv, b_cst], writes=[b_sv])
         sch.op("vector", lambda e: e.reciprocal(sv[:, 0:1], sv[:, 6:7]),
                reads=[b_sv], writes=[b_sv])
         sch.op("vector", lambda e: e.tensor_scalar_mul(
             sv[:, 1:2], mv[:, 0:1], -1.0), reads=[b_mv], writes=[b_sv])
         sch.op("vector", lambda e, sl=sl: e.tensor_scalar(
             xnt[:, :], xt[sl][:, :], sv[:, 1:2], sv[:, 0:1],
             ALU.add, ALU.mult),
             reads=[b_xt[sl], b_sv], writes=[b_xnt])
        ch, tl = t // (SC // 128), t % (SC // 128)
        for j0 in (range(0, ND, 4) if "T" in P0DBG else ()):
            ng = min(4, ND - j0)
            kk = trc[0]
            trc[0] += 1
            bb = kk % 2
            for jj in range(ng):
                sch.op("tensor", lambda e, j=j0 + jj, jj=jj, bb=bb:
                       e.transpose(
                           ps_tr[:, bb * 512 + jj * 128:
                                 bb * 512 + jj * 128 + 128],
                           xnt[:, t128(j)], ident[:, :]),
                       reads=[b_xnt, b_ident],
                       writes=[b_tr[bb]] if jj == 0 else [],
                       writes_nodep=[] if jj == 0 else [b_tr[bb]])
            if "S" in P0DBG:
                srcv = ps_tr[:, bb * 512:bb * 512 + ng * 128].rearrange(
                    "p (g c) -> p g c", c=128)
                off = j0 * SC + tl * 128
                dims = [[ND * SC, 128], [SC, ng], [1, 128]]
                dsth = bass.AP(stg_h[0], off, dims)
                dstl = bass.AP(stg_h[1], off, dims)
                sch.op("scalar", lambda e, srcv=srcv, dsth=dsth:
                       e.copy(dsth, srcv),
                       reads=[b_tr[bb]], writes=[b_stg[0]])
                sch.op("vector", lambda e, srcv=srcv, dsth=dsth, dstl=dstl:
                       e.scalar_tensor_tensor(dstl, srcv, 1.0, dsth,
                                              ALU.mult, ALU.subtract),
                       reads=[b_tr[bb], b_stg[0]], writes=[b_stg[1]])
        if tl == SC // 128 - 1 and "W" in P0DBG:
            for p in range(2):
                sch.op("scalar", lambda e, p=p, ch=ch: e.dma_start(
                    d_xn[p][t128(ch), :], stg[p][:, :]),
                    reads=[b_stg[p]], writes=[b_xn_dram[p][ch]], dma=True)
        pp1_step(128 // NTS + 1)

    if not NOPP:
        pp1_flush()
        prepass_fin(tot1, 0, DFF * D)
    else:
        sch.op("vector", lambda e: e.memset(thr[:, 0:4], 0.1),
               writes=[b_thr])
    if _cut(2):
        return nc

    sch.set_barrier()
    f1 = Arena(nc, PP_END, SB_LIMIT, "f1")
    xnf = [f1.alloc(f"xnf{p}", [128, NCH * ND * SC], BF16) for p in (0, 1)]
    w1s = [f1.alloc(f"w1s{i}", [128, ND * 128], F32) for i in (0, 1)]
    w1m = [f1.alloc(f"w1m{i}", [128, ND * 128], BF16) for i in (0, 1)]
    w1q = [f1.alloc(f"w1q{i}", [128, ND * 128], BF16) for i in (0, 1)]
    gt = [f1.alloc(f"gt{i}", [128, SC], F32) for i in (0, 1)]
    gst = [[f1.alloc(f"gst{i}{p}", [128, SC], BF16) for p in (0, 1)]
           for i in (0, 1)]
    b_xnf = [sch.buf(), sch.buf()]
    b_w1s = [sch.buf(), sch.buf()]
    b_w1m = [sch.buf(), sch.buf()]
    b_w1q = [sch.buf(), sch.buf()]
    b_gt = [sch.buf(), sch.buf()]
    b_gst = [[sch.buf(), sch.buf()], [sch.buf(), sch.buf()]]

    for p in range(2):
        for ch in range(NCH):
            sch.op("sync", lambda e, p=p, ch=ch: e.dma_start(
                xnf[p][:, ch * ND * SC:(ch + 1) * ND * SC],
                d_xn[p][t128(ch), :]),
                reads=[b_xn_dram[p][ch]], writes=[b_xnf[p]], dma=True)

    if NOPP:
        pp2_step = pp2_flush = lambda *a: None
        tot2 = 0
    else:
        pp2_step, pp2_flush, tot2 = prepass_stream(d_w2t, DFF)

    def w1_fetch_quant(ft):
        sl = ft % 2
        sch.op("sync", lambda e, ft=ft, sl=sl: e.dma_start(
            w1s[sl][:, :], d_w1p[t128(ft), :]),
            writes=[b_w1s[sl]], dma=True)
        sch.op("vector", lambda e, sl=sl: e.tensor_single_scalar(
            w1m[sl][:, :], w1s[sl][:, :], thr[:, 1:2], ALU.is_le),
            reads=[b_w1s[sl], b_thr], writes=[b_w1m[sl]])
        sch.op("vector", lambda e, sl=sl: e.scalar_tensor_tensor(
            w1q[sl][:, :], w1s[sl][:, :], thr[:, 0:1], w1m[sl][:, :],
            ALU.is_ge, ALU.subtract),
            reads=[b_w1s[sl], b_w1m[sl], b_thr], writes=[b_w1q[sl]])

    w1_fetch_quant(0)
    for ft in range(NF):
        sl = ft % 2
        if ft + 1 < NF:
            w1_fetch_quant(ft + 1)
        pp2_step(128 // NF + 2 if NF >= 32 else 64)
        for ch in range(NCH):
            pm = (ft * NCH + ch) % 2
            first = True
            for j in range(ND):
                for p in range(2):
                    base = ch * ND * SC + j * SC
                    sch.op("tensor", lambda e, j=j, p=p, pm=pm, base=base,
                           sl=sl, first=first, last=(j == ND - 1 and p == 1):
                           e.matmul(
                               ps_m[pm][:, 0:SC], w1q[sl][:, t128(j)],
                               xnf[p][:, base:base + SC],
                               start=first, stop=last),
                           reads=[b_w1q[sl], b_xnf[p]],
                           writes=[b_m[pm]] if first else [],
                           writes_nodep=[] if first else [b_m[pm]])
                    first = False
            gsl = (ft * NCH + ch) % 2
            sch.op("scalar", lambda e, ft=ft, pm=pm, gsl=gsl: e.activation(
                gt[gsl][:, :], ps_m[pm][:, 0:SC], AF.Gelu,
                bias=b1c[:, ft:ft + 1], scale=1.0),
                reads=[b_m[pm], b_b1c], writes=[b_gt[gsl]])
            sch.op("scalar", lambda e, ft=ft, pm=pm, gsl=gsl:
                   e.activation(gst[gsl][0][:, :], ps_m[pm][:, 0:SC],
                                AF.Gelu, bias=b1c[:, ft:ft + 1], scale=1.0),
                   reads=[b_m[pm], b_b1c], writes=[b_gst[gsl][0]])
            sch.op("vector", lambda e, gsl=gsl: e.scalar_tensor_tensor(
                gst[gsl][1][:, :], gt[gsl][:, :], 1.0, gst[gsl][0][:, :],
                ALU.mult, ALU.subtract),
                reads=[b_gt[gsl], b_gst[gsl][0]], writes=[b_gst[gsl][1]])
            for p in range(2):
                sch.op("scalar", lambda e, p=p, ft=ft, ch=ch, gsl=gsl:
                       e.dma_start(d_g[p][t128(ft), ch * SC:(ch + 1) * SC],
                                   gst[gsl][p][:, :]),
                       reads=[b_gst[gsl][p]], writes=[b_g_dram[p]], dma=True)

    if _cut(3):
        return nc


    if not NOPP:
        pp2_flush()
        prepass_fin(tot2, 1, DFF * D)

    sch.set_barrier()
    aa = Arena(nc, PERS_END, SB_LIMIT, "att")
    xna = [aa.alloc(f"xna{p}", [128, ND * SC], BF16) for p in (0, 1)]
    WCOLS = 3 * ND * 128
    PW = min(256, WCOLS)
    wqs = [aa.alloc(f"wqs{i}", [128, PW], F32) for i in (0, 1)]
    wsl = [[aa.alloc(f"wsl{par}_{p}", [128, G * WCOLS], BF16)
            for p in (0, 1)] for par in (0, 1)]
    qh = [[aa.alloc(f"q{pq}_{p}", [128, S], BF16) for p in (0, 1)]
          for pq in range(2)]
    kh = [[aa.alloc(f"k{pq}_{p}", [128, S], BF16) for p in (0, 1)]
          for pq in range(2)]
    vh = [[aa.alloc(f"v{pq}_{p}", [128, S], BF16) for p in (0, 1)]
          for pq in range(2)]
    vf = [aa.alloc("vf0", [128, SC], F32)]
    af32 = [aa.alloc(f"af32_{i}", [128, SBW], F32) for i in range(3)]
    ath_h = [aa.alloc(f"at_{p}", [128, NTS * QC], BF16) for p in (0, 1)]
    ath = ath_h
    yst = [[aa.alloc(f"yst{i}{p}", [128, QC], BF16) for p in (0, 1)]
           for i in (0,)]
    asv = aa.alloc("asv", [128, 32], F32)
    b_xna = [sch.buf(), sch.buf()]
    b_wqs = [sch.buf(), sch.buf()]
    b_wsl = [[sch.buf(), sch.buf()] for _ in (0, 1)]
    b_qh = [[sch.buf(), sch.buf()] for _ in range(2)]
    b_kh = [[sch.buf(), sch.buf()] for _ in range(2)]
    b_vh = [[sch.buf(), sch.buf()] for _ in range(2)]
    pmg = [0]
    b_vf = [sch.buf()]
    b_af32 = [sch.buf() for _ in range(3)]
    b_ath = [sch.buf(), sch.buf()]
    b_yst = [[sch.buf(), sch.buf()]]
    b_asv = [sch.buf(), sch.buf()]

    def fill_wgroup(grp):
        par = grp % 2
        pieces = [(o, min(PW, WCOLS - o)) for o in range(0, WCOLS, PW)]
        for h2 in range(G):
            head = grp * G + h2
            for pc, (po, pw_) in enumerate(pieces):
                wsl2 = (h2 * len(pieces) + pc) % 2
                sch.op("sync", lambda e, head=head, po=po, pw_=pw_, wsl2=wsl2:
                       e.dma_start(wqs[wsl2][:, 0:pw_],
                                   d_wqkv[t128(head), po:po + pw_]),
                       writes=[b_wqs[wsl2]], dma=True)
                co = h2 * WCOLS + po
                sch.op("gpsimd", lambda e, co=co, pw_=pw_, wsl2=wsl2, par=par:
                       e.tensor_copy(wsl[par][0][:, co:co + pw_],
                                     wqs[wsl2][:, 0:pw_]),
                       reads=[b_wqs[wsl2]], writes=[b_wsl[par][0]])
                sch.op("gpsimd", lambda e, co=co, pw_=pw_, wsl2=wsl2, par=par:
                       e.tensor_tensor(
                           wsl[par][1][:, co:co + pw_], wqs[wsl2][:, 0:pw_],
                           wsl[par][0][:, co:co + pw_], ALU.subtract),
                       reads=[b_wqs[wsl2], b_wsl[par][0]],
                       writes=[b_wsl[par][1]])

    fill_wgroup(0)
    for grp in range(H // G):

        par = grp % 2

        def mk_proj_tasks(g2):
            pq = g2 % 2
            tasks = []

            def wap(p, h2, tgt, j, pq=pq):
                co = h2 * WCOLS + (tgt * ND + j) * 128
                return wsl[pq][p][:, co:co + 128]

            for ch in range(NCH):
                def t_load(ch=ch):
                    for p in range(2):
                        sch.op("sync", lambda e, p=p, ch=ch: e.dma_start(
                            xna[p][:, :], d_xn[p][t128(ch), :]),
                            reads=[b_xn_dram[p][ch]], writes=[b_xna[p]],
                            dma=True)
                tasks.append(t_load)
                for tgt in range(3):
                    def t_grp(ch=ch, tgt=tgt, pq=pq, wap=wap):
                        pm = pmg[0] % 2
                        pmg[0] += 1
                        h2 = 0
                        first = True
                        for j in range(ND):
                            for (pw, px) in ((0, 0), (0, 1), (1, 0)):
                                last = (j == ND - 1 and (pw, px) == (1, 0))
                                sch.op("tensor",
                                       lambda e, pw=pw, px=px, tgt=tgt,
                                       j=j, pm=pm, first=first, last=last,
                                       wap=wap: e.matmul(
                                           ps_m[pm][:, 0:SC],
                                           wap(pw, 0, tgt, j),
                                           xna[px][:, j * SC:j * SC + SC],
                                           start=first, stop=last),
                                       reads=[b_wsl[pq][pw], b_xna[px]],
                                       writes=[b_m[pm]] if first else [],
                                       writes_nodep=[] if first
                                       else [b_m[pm]])
                                first = False
                        if tgt < 2:
                            dst = qh[pq] if tgt == 0 else kh[pq]
                            dbuf = b_qh[pq] if tgt == 0 else b_kh[pq]
                            split_psum(ps_m[pm][:, 0:SC],
                                       dst[0][:, ch * SC:(ch + 1) * SC],
                                       dst[1][:, ch * SC:(ch + 1) * SC],
                                       b_m[pm], dbuf[0], dbuf[1])
                        else:
                            vsl = 0
                            sch.op("vector", lambda e, pm=pm, vsl=vsl:
                                   e.tensor_copy(vf[vsl][:, :],
                                                 ps_m[pm][:, 0:SC]),
                                   reads=[b_m[pm]], writes=[b_vf[vsl]])
                            for tv0 in range(0, SC // 128, 4):
                                ngv = min(4, SC // 128 - tv0)
                                kk = trc[0]
                                trc[0] += 1
                                bb = kk % 2
                                for jj in range(ngv):
                                    sch.op("tensor",
                                           lambda e, tv=tv0 + jj, jj=jj,
                                           bb=bb, vsl=vsl: e.transpose(
                                               ps_tr[:, bb * 512 + jj * 128:
                                                     bb * 512 + jj * 128
                                                     + 128],
                                               vf[vsl][:, t128(tv)],
                                               ident[:, :]),
                                           reads=[b_vf[vsl], b_ident],
                                           writes=[b_tr[bb]] if jj == 0
                                           else [],
                                           writes_nodep=[] if jj == 0
                                           else [b_tr[bb]])
                                co = (ch * (SC // 128) + tv0) * 128
                                wv = ngv * 128
                                split_psum(ps_tr[:, bb * 512:bb * 512 + wv],
                                           vh[pq][0][:, co:co + wv],
                                           vh[pq][1][:, co:co + wv],
                                           b_tr[bb], b_vh[pq][0],
                                           b_vh[pq][1])
                    tasks.append(t_grp)
            return tasks

        if grp == 0:
            for t in mk_proj_tasks(0):
                t()
        if grp + 1 < H // G:
            fill_wgroup(grp + 1)
        feeder = mk_proj_tasks(grp + 1) if grp + 1 < H // G else []

        def feed():
            if feeder:
                feeder.pop(0)()

        for h2 in range(G):
            head = grp * G + h2
            for til in range(QT):
                for r in range(til + 1, QT):
                    for p in range(2):
                        off = r * QC + til * 128
                        dims = [[NTS * QC, 128], [QT * QC, NTS // QT],
                                [1, 128]]
                        ap = bass.AP(ath_h[p], off, dims)
                        sch.op("vector", lambda e, ap=ap: e.memset(ap, 0.0),
                               writes=[b_ath[p]])
            pending_tr = []
            pending_av = [None]

            def flush_tr():
                for emit in pending_tr:
                    emit()
                pending_tr.clear()

            for qc in range(NQC):
                KT = (qc + 1) * QT
                for ti in range(QT):
                    ig = qc * QT + ti
                    klen = (ig + 1) * 128
                    nm = (klen + 511) // 512
                    p2 = ti % 2
                    B = 16 * p2
                    sb = b_asv[p2]
                    asl = ig % 3
                    for m in range(nm):
                        w = min(512, klen - m * 512)
                        slot = sb_cnt[0] % 4
                        sb_cnt[0] += 1
                        for (pq, pk) in ((0, 0), (0, 1), (1, 0)):
                            first = (pq, pk) == (0, 0)
                            sch.op("tensor",
                                   lambda e, pq=pq, pk=pk, par=par,
                                   ig=ig, m=m,
                                   slot=slot, w=w, first=first: e.matmul(
                                       ps_big[:, slot * 512:slot * 512 + w],
                                       qh[par][pq][:, t128(ig)],
                                       kh[par][pk][:, m * 512:m * 512 + w],
                                       start=first,
                                       stop=(pq, pk) == (1, 0)),
                                   reads=[b_qh[par][pq], b_kh[par][pk]],
                                   writes=[b_sb4[slot]] if first else [],
                                   writes_nodep=[] if first else
                                   [b_sb4[slot]])
                        sch.op("vector", lambda e, m=m, w=w, slot=slot,
                               asl=asl: e.tensor_copy(
                                   af32[asl][:, m * 512:m * 512 + w],
                                   ps_big[:, slot * 512:slot * 512 + w]),
                               reads=[b_sb4[slot]], writes=[b_af32[asl]])
                        if m < nm - 1:
                            sch.op("vector", lambda e, m=m, w=w, B=B, asl=asl:
                                   e.reduce_max(
                                       asv[:, B + m:B + m + 1],
                                       af32[asl][:, m * 512:m * 512 + w],
                                       AX.X),
                                   reads=[b_af32[asl]], writes=[sb])
                    sch.op("vector", lambda e, ig=ig, klen=klen, asl=asl:
                           e.tensor_tensor(af32[asl][:, ig * 128:klen],
                                           af32[asl][:, ig * 128:klen],
                                           mask[:, :], ALU.add),
                           reads=[b_af32[asl], b_mask], writes=[b_af32[asl]])
                    m = nm - 1
                    w = klen - m * 512
                    sch.op("vector", lambda e, m=m, w=w, B=B, asl=asl:
                           e.reduce_max(asv[:, B + m:B + m + 1],
                                        af32[asl][:, m * 512:m * 512 + w],
                                        AX.X),
                           reads=[b_af32[asl]], writes=[sb])
                    sch.op("vector", lambda e, nm=nm, B=B: e.reduce_max(
                        asv[:, B + 4:B + 5], asv[:, B:B + nm], AX.X),
                        reads=[sb], writes=[sb])
                    sch.op("vector", lambda e, B=B: e.tensor_scalar_mul(
                        asv[:, B + 5:B + 6], asv[:, B + 4:B + 5], -scale),
                        reads=[sb], writes=[sb])
                    sch.op("scalar", lambda e, klen=klen, asl=asl, B=B:
                           e.activation(af32[asl][:, 0:klen],
                                        af32[asl][:, 0:klen], AF.Exp,
                                        bias=asv[:, B + 5:B + 6],
                                        scale=cst[:, 1:2],
                                        accum_out=asv[:, B + 6:B + 7]),
                           reads=[sb, b_cst],
                           writes=[b_af32[asl], sb])
                    sch.op("vector", lambda e, B=B: e.reciprocal(
                        asv[:, B + 7:B + 8], asv[:, B + 6:B + 7]),
                        reads=[sb], writes=[sb])
                    sch.op("vector", lambda e, klen=klen, asl=asl, B=B:
                           e.tensor_scalar_mul(af32[asl][:, 0:klen],
                                               af32[asl][:, 0:klen],
                                               asv[:, B + 7:B + 8]),
                           reads=[sb], writes=[b_af32[asl]])

                    def mk_tr(ig=ig, ti=ti, asl=asl):
                        def emit():
                            for j0 in range(0, ig + 1, 4):
                                ng = min(4, ig + 1 - j0)
                                k = trc[0]
                                trc[0] += 1
                                b = k % 2
                                for jj in range(ng):
                                    sch.op("tensor",
                                           lambda e, j=j0 + jj, jj=jj, b=b,
                                           asl=asl: e.transpose(
                                               ps_tr[:, b * 512 + jj * 128:
                                                     b * 512 + jj * 128 + 128],
                                               af32[asl][:, t128(j)],
                                               ident[:, :]),
                                           reads=[b_af32[asl], b_ident],
                                           writes=[b_tr[b]] if jj == 0 else [],
                                           writes_nodep=[] if jj == 0
                                           else [b_tr[b]])
                                srcv = ps_tr[:, b * 512:b * 512 + ng * 128
                                             ].rearrange("p (g c) -> p g c",
                                                         c=128)
                                off = j0 * QC + ti * 128
                                dims = [[NTS * QC, 128], [QC, ng], [1, 128]]
                                dsth = bass.AP(ath_h[0], off, dims)
                                dstl = bass.AP(ath_h[1], off, dims)
                                sch.op("scalar",
                                       lambda e, srcv=srcv, dsth=dsth:
                                       e.copy(dsth, srcv),
                                       reads=[b_tr[b]], writes=[b_ath[0]])
                                sch.op("vector",
                                       lambda e, srcv=srcv, dsth=dsth,
                                       dstl=dstl: e.scalar_tensor_tensor(
                                           dstl, srcv, 1.0, dsth,
                                           ALU.mult, ALU.subtract),
                                       reads=[b_tr[b], b_ath[0]],
                                       writes=[b_ath[1]])
                        return emit

                    if ti == 0 and pending_av[0] is not None:
                        flush_tr()
                        pending_av[0]()
                        pending_av[0] = None
                    pending_tr.append(mk_tr())
                    if len(pending_tr) > 2:
                        pending_tr.pop(0)()
                    feed()

                def mk_av(qc=qc, KT=KT, par=par, head=head):
                    def emit():
                        pm = pmg[0] % 2
                        pmg[0] += 1
                        first = True
                        for j in range(KT):
                            for (pv, pa) in ((0, 0), (0, 1), (1, 0)):
                                last = (j == KT - 1 and (pv, pa) == (1, 0))
                                sch.op("tensor",
                                       lambda e, pv=pv, pa=pa, par=par,
                                       j=j, pm=pm, first=first, last=last:
                                       e.matmul(
                                           ps_m[pm][:, 0:QC],
                                           vh[par][pv][:, t128(j)],
                                           ath[pa][:, j * QC:(j + 1) * QC],
                                           start=first, stop=last),
                                       reads=[b_vh[par][pv], b_ath[pa]],
                                       writes=[b_m[pm]] if first else [],
                                       writes_nodep=[] if first
                                       else [b_m[pm]])
                                first = False
                        ysl = 0
                        split_psum_v(ps_m[pm][:, 0:QC], yst[ysl][0][:, :],
                                     yst[ysl][1][:, :], b_m[pm],
                                     b_yst[ysl][0], b_yst[ysl][1])
                        for p in range(2):
                            sch.op("scalar",
                                   lambda e, p=p, head=head, qc=qc, ysl=ysl:
                                   e.dma_start(d_y[p][t128(head),
                                                      qc * QC:(qc + 1) * QC],
                                               yst[ysl][p][:, :]),
                                   reads=[b_yst[ysl][p]],
                                   writes=[b_y_dram[p]], dma=True)
                    return emit

                pending_av[0] = mk_av()
            flush_tr()
            if pending_av[0] is not None:
                pending_av[0]()
                pending_av[0] = None
            while feeder:
                feeder.pop(0)()

    if _cut(4):
        return nc

    sch.set_barrier()
    acc = nc.alloc_sbuf_tensor_at("acc", [128, NTS * D], F32, offset=ACC_OFF)
    op = Arena(nc, PERS_END, ACC_OFF, "op")
    SG = min(2, NTS)
    ysl_ = [[op.alloc(f"ysl{j}_{p}", [128, SG * 128], BF16) for p in (0, 1)]
            for j in range(ND)]
    wos = [op.alloc(f"wos{i}", [128, EC], F32) for i in (0, 1)]
    wol = [[op.alloc(f"wol{j}_{p}", [128, EC], BF16) for p in (0, 1)]
           for j in range(ND)]
    xot = [op.alloc(f"xot{i}", [128, EC], F32) for i in (0, 1)]
    b_acc = [sch.buf() for _ in range(NTS)]
    b_ysl = [[sch.buf(), sch.buf()] for _ in range(ND)]
    b_wos = [sch.buf(), sch.buf()]
    b_wol = [[sch.buf(), sch.buf()] for _ in range(ND)]
    b_xot = [sch.buf(), sch.buf()]

    for ec in range(NEC):
        for j in range(ND):
            wsl2 = j % 2
            sch.op("sync", lambda e, j=j, ec=ec, wsl2=wsl2: e.dma_start(
                wos[wsl2][:, :], d_wout[t128(j), ec * EC:(ec + 1) * EC]),
                writes=[b_wos[wsl2]], dma=True)
            sch.op("scalar", lambda e, j=j, wsl2=wsl2: e.copy(
                wol[j][0][:, :], wos[wsl2][:, :]),
                reads=[b_wos[wsl2]], writes=[b_wol[j][0]])
            sch.op("vector", lambda e, j=j, wsl2=wsl2:
                   e.scalar_tensor_tensor(
                       wol[j][1][:, :], wos[wsl2][:, :], 1.0,
                       wol[j][0][:, :], ALU.mult, ALU.subtract),
                   reads=[b_wos[wsl2], b_wol[j][0]], writes=[b_wol[j][1]])
        for sg in range(NTS // SG):
            for j in range(ND):
                for p in range(2):
                    sch.op("sync", lambda e, j=j, p=p, sg=sg: e.dma_start(
                        ysl_[j][p][:, :],
                        d_y[p][t128(j), sg * SG * 128:(sg + 1) * SG * 128]),
                        reads=[b_y_dram[p]], writes=[b_ysl[j][p]], dma=True)
            for st2 in range(SG):
                st = sg * SG + st2
                pm = (ec * NTS + sg * SG + st2) % 2
                first = True
                for j in range(ND):
                    for (py, pw) in ((0, 0), (0, 1), (1, 0)):
                        last = (j == ND - 1 and (py, pw) == (1, 0))
                        sch.op("tensor",
                               lambda e, py=py, pw=pw, j=j, st2=st2, pm=pm,
                               first=first, last=last: e.matmul(
                                   ps_m[pm][:, 0:EC],
                                   ysl_[j][py][:, t128(st2)],
                                   wol[j][pw][:, :],
                                   start=first, stop=last),
                               reads=[b_ysl[j][py], b_wol[j][pw]],
                               writes=[b_m[pm]] if first else [],
                               writes_nodep=[] if first else [b_m[pm]])
                        first = False
                xsl = st2 % 2
                sch.op("scalar", lambda e, st=st, ec=ec, xsl=xsl: e.dma_start(
                    xot[xsl][:, :], d_x[t128(st), ec * EC:(ec + 1) * EC]),
                    reads=[b_xot[xsl]], writes=[b_xot[xsl]], dma=True)
                sch.op("vector", lambda e, st=st, ec=ec, pm=pm, xsl=xsl:
                       e.scalar_tensor_tensor(
                           acc[:, st * D + ec * EC:st * D + (ec + 1) * EC],
                           ps_m[pm][:, 0:EC], 1.0, xot[xsl][:, :],
                           ALU.mult, ALU.add),
                       reads=[b_m[pm], b_xot[xsl]], writes=[b_acc[st]])

    if _cut(5):
        return nc
    sch.set_barrier()
    f2 = Arena(nc, PERS_END, ACC_OFF, "f2")
    FCn = NF // FC
    gsl_ = [[f2.alloc(f"gsl{f}_{p}", [128, S], BF16) for p in (0, 1)]
            for f in range(FC)]
    w2s = [f2.alloc("w2s0", [128, D], F32)]
    w2m = [f2.alloc("w2m0", [128, D], BF16)]
    w2q = [f2.alloc(f"w2q{f}", [128, D], BF16) for f in range(2 * FC)]
    b_gsl = [[[sch.buf(), sch.buf()] for _ in range(2)]
             for _ in range(FC)]
    b_w2s = [sch.buf()]
    b_w2m = [sch.buf()]
    b_w2q = [sch.buf() for _ in range(2 * FC)]

    def w2_fetch_quant(fc):
        cpar = fc % 2
        for f in range(FC):
            ft = fc * FC + f
            fq = cpar * FC + f
            sch.op("sync", lambda e, ft=ft: e.dma_start(
                w2s[0][:, :], d_w2t[t128(ft), :]),
                reads=[b_w2s[0]], writes=[b_w2s[0]], dma=True)
            sch.op("vector", lambda e: e.tensor_single_scalar(
                w2m[0][:, :], w2s[0][:, :], thr[:, 3:4], ALU.is_le),
                reads=[b_w2s[0], b_thr], writes=[b_w2m[0]])
            sch.op("vector", lambda e, fq=fq:
                   e.scalar_tensor_tensor(
                       w2q[fq][:, :], w2s[0][:, :], thr[:, 2:3],
                       w2m[0][:, :], ALU.is_ge, ALU.subtract),
                   reads=[b_w2s[0], b_w2m[0], b_thr],
                   writes=[b_w2q[fq]])

    HS = S // 2

    def g_load(fc, half):
        for f in range(FC):
            ft = fc * FC + f
            for p in range(2):
                sch.op("sync", lambda e, ft=ft, f=f, p=p, half=half:
                       e.dma_start(
                           gsl_[f][p][:, half * HS:(half + 1) * HS],
                           d_g[p][t128(ft), half * HS:(half + 1) * HS]),
                       reads=[b_g_dram[p], b_gsl[f][half][p]],
                       writes=[b_gsl[f][half][p]], dma=True)

    w2_fetch_quant(0)
    g_load(0, 0)
    g_load(0, 1)
    for fc in range(FCn):
        cpar = fc % 2
        if fc + 1 < FCn:
            w2_fetch_quant(fc + 1)
        for st in range(NTS):
            half = (st * 128) // HS
            if st == NTS // 2 and fc + 1 < FCn:
                g_load(fc + 1, 0)
            for ec in range(NEC):
                pm = (st * NEC + ec) % 2
                first = True
                for f in range(FC):
                    fq = cpar * FC + f
                    for p in range(2):
                        last = (f == FC - 1 and p == 1)
                        sch.op("tensor",
                               lambda e, f=f, fq=fq, p=p, st=st, ec=ec,
                               pm=pm, first=first, last=last: e.matmul(
                                   ps_m[pm][:, 0:EC],
                                   gsl_[f][p][:, t128(st)],
                                   w2q[fq][:, ec * EC:(ec + 1) * EC],
                                   start=first, stop=last),
                               reads=[b_gsl[f][half][p], b_w2q[fq]],
                               writes=[b_m[pm]] if first else [],
                               writes_nodep=[] if first else [b_m[pm]])
                        first = False
                sch.op("vector", lambda e, st=st, ec=ec, pm=pm:
                       e.scalar_tensor_tensor(
                           acc[:, st * D + ec * EC:st * D + (ec + 1) * EC],
                           ps_m[pm][:, 0:EC], 1.0,
                           acc[:, st * D + ec * EC:st * D + (ec + 1) * EC],
                           ALU.mult, ALU.add),
                       reads=[b_m[pm]], writes=[b_acc[st]])
        if fc + 1 < FCn:
            g_load(fc + 1, 1)

    if _cut(6):
        return nc
    sch.set_barrier()
    ea = Arena(nc, PERS_END, ACC_OFF, "ea")
    b2row = ea.alloc("b2row", [1, D], F32)
    b2b = ea.alloc("b2b", [128, D], F32)
    ost = [ea.alloc(f"ost{i}", [128, EC], F32) for i in (0, 1)]
    b_b2row, b_b2b = sch.buf(), sch.buf()
    b_ost = [sch.buf(), sch.buf()]

    sch.op("sync", lambda e: e.dma_start(b2row[0:1, :], d_b2r),
           writes=[b_b2row], dma=True)
    for ec in range(NEC):
        pm = ec % 2
        sch.op("tensor", lambda e, ec=ec, pm=pm: e.matmul(
            ps_m[pm][:, 0:EC], ones[0:1, 0:128],
            b2row[0:1, ec * EC:(ec + 1) * EC], start=True, stop=True),
            reads=[b_b2row, b_ones], writes=[b_m[pm]])
        sch.op("vector", lambda e, ec=ec, pm=pm: e.tensor_copy(
            b2b[:, ec * EC:(ec + 1) * EC], ps_m[pm][:, 0:EC]),
            reads=[b_m[pm]], writes=[b_b2b])
    for st in range(NTS):
        for ec in range(NEC):
            osl = (st * NEC + ec) % 2
            sch.op("vector", lambda e, st=st, ec=ec, osl=osl:
                   e.scalar_tensor_tensor(
                       ost[osl][:, :],
                       acc[:, st * D + ec * EC:st * D + (ec + 1) * EC],
                       1.0, b2b[:, ec * EC:(ec + 1) * EC],
                       ALU.mult, ALU.add),
                   reads=[b_acc[st], b_b2b, b_ost[osl]], writes=[b_ost[osl]])
            sch.op("scalar", lambda e, st=st, ec=ec, osl=osl: e.dma_start(
                d_out[t128(st), ec * EC:(ec + 1) * EC], ost[osl][:, :]),
                reads=[b_ost[osl]], writes=[b_out_dram], dma=True)

    sch.finalize()
    sch.emit()
    return nc


def prep_weights(cfg, w_qkv, w_out, w1, b1, w2, b2):
    D, H, DFF = cfg["D"], cfg["H"], cfg["DFF"]
    ND, NF = D // 128, DFF // 128
    w4 = np.asarray(w_qkv, np.float32).reshape(3, H, 128, ND, 128)
    wqkv = np.ascontiguousarray(np.transpose(w4, (1, 4, 0, 3, 2))
                                ).reshape(H * 128, 3 * ND * 128)
    woutT = np.ascontiguousarray(np.asarray(w_out, np.float32).T)
    w1r = np.asarray(w1, np.float32).reshape(NF, 128, ND, 128)
    w1p = np.ascontiguousarray(np.transpose(w1r, (0, 3, 2, 1))
                               ).reshape(NF * 128, D)
    w2t = np.ascontiguousarray(np.asarray(w2, np.float32).T)
    b1c = np.ascontiguousarray(np.asarray(b1, np.float32).reshape(NF, 128).T)
    b2r = np.ascontiguousarray(np.asarray(b2, np.float32).reshape(1, D))
    identc = np.eye(128, dtype=np.float32)
    idx = np.arange(128)
    maskneg = np.where(idx[None, :] <= idx[:, None], 0.0, NEG_BIG
                       ).astype(np.float32)
    return dict(wqkv=wqkv, woutT=woutT, w1p=w1p, w2t=w2t, b1c=b1c, b2r=b2r,
                identc=identc, maskneg=maskneg)


def kernel(L0_attn_x, L0_attn_w_qkv, L0_attn_w_out, L0_ffn_w1, L0_ffn_b1,
           L0_ffn_w2, L0_ffn_b2):
    cfg = default_cfg()
    n = cfg["n_cores"]
    x = np.ascontiguousarray(np.asarray(L0_attn_x, np.float32))
    wd = prep_weights(cfg, L0_attn_w_qkv, L0_attn_w_out, L0_ffn_w1,
                      L0_ffn_b1, L0_ffn_w2, L0_ffn_b2)
    nc = build_program(cfg)
    in_maps = [dict(x=x[c], **wd) for c in range(n)]
    res = run_bass_kernel_spmd(nc, in_maps, core_ids=list(range(n)))
    return np.stack([np.asarray(r["out"], np.float32)
                     for r in res.results], axis=0)


# version 31
# speedup vs baseline: 1.0568x; 1.0454x over previous
import numpy as np
import concourse.bass as bass
import concourse.mybir as mybir
from concourse.bass_utils import run_bass_kernel_spmd

F32 = mybir.dt.float32
BF16 = mybir.dt.bfloat16
AF = mybir.ActivationFunctionType
ALU = mybir.AluOpType
AX = mybir.AxisListType

NEG_BIG = -1.0e30
LN_EPS = 1.0e-5


def default_cfg():
    return dict(S=2048, D=2048, H=16, DFF=8192, G=1, SC=512, QC=512, EC=512,
                n_cores=8)


class Buf:
    __slots__ = ("w", "r", "dsem")

    def __init__(self, init=None):
        self.w = dict(init) if init else {}
        self.r = dict(init) if init else {}
        self.dsem = None


class Sched:
    COMPUTE_SEM = {"tensor": "T", "vector": "V", "scalar": "A", "gpsimd": "G"}
    QUEUE_SEM = {"sync": "qS", "scalar": "qA", "gpsimd": "qG"}
    ENGINES = ("tensor", "vector", "scalar", "gpsimd", "sync")

    def __init__(self, nc):
        self.nc = nc
        self.ops = {e: [] for e in self.ENGINES}
        self.sem = {}
        self.cnt = {}
        for s in ("T", "V", "A", "G", "qS", "qA", "qG"):
            self.sem[s] = nc.alloc_semaphore("sem_" + s)
            self.cnt[s] = 0

    def snap(self):
        return dict(self.cnt)

    def buf(self):
        return Buf(getattr(self, "_snap", None))

    def set_barrier(self):
        self._snap = dict(self.cnt)

    def op(self, engine, fn, reads=(), writes=(), writes_nodep=(),
           war=(), dma=False):
        need = {}
        for b in reads:
            for s, v in b.w.items():
                if need.get(s, 0) < v:
                    need[s] = v
        for b in war:
            for s, v in b.r.items():
                if need.get(s, 0) < v:
                    need[s] = v
        for b in writes:
            for d in (b.w, b.r):
                for s, v in d.items():
                    if need.get(s, 0) < v:
                        need[s] = v
        if dma:
            tb = (list(writes) + list(writes_nodep))[0]
            if tb.dsem is None:
                tb.dsem = f"qd{len(self.sem)}"
                self.sem[tb.dsem] = self.nc.alloc_semaphore(tb.dsem)
                self.cnt[tb.dsem] = 0
            s = tb.dsem
        else:
            s = self.COMPUTE_SEM[engine]
        amt = 16 if dma else 1
        self.cnt[s] += amt
        tok = (s, self.cnt[s])
        self.ops[engine].append((need, fn, s, amt))
        for b in list(writes) + list(writes_nodep):
            if b.w.get(s, 0) < tok[1]:
                b.w[s] = tok[1]
        for b in reads:
            if b.r.get(s, 0) < tok[1]:
                b.r[s] = tok[1]
        return tok

    def finalize(self):
        final = dict(self.cnt)
        self.ops["sync"].append((final, None, None, 0))

    def emit(self):
        nc = self.nc
        with nc.Block() as block:
            def runner(engine):
                oplist = self.ops[engine]
                sems = self.sem

                def go(eng):
                    wm = {}
                    for need, fn, s, amt in oplist:
                        for ns in sorted(need):
                            v = need[ns]
                            if wm.get(ns, 0) < v:
                                eng.wait_ge(sems[ns], v)
                                wm[ns] = v
                        if fn is not None:
                            inst = fn(eng)
                            inst.then_inc(sems[s], amt)
                return go

            block.tensor(runner("tensor"))
            block.vector(runner("vector"))
            block.scalar(runner("scalar"))
            block.gpsimd(runner("gpsimd"))
            block.sync(runner("sync"))


class Arena:

    def __init__(self, nc, base, limit, tag):
        self.nc = nc
        self.off = base
        self.limit = limit
        self.tag = tag
        self.n = 0

    def alloc(self, name, shape, dtype):
        nbytes = shape[1] * mybir.dt.size(dtype)
        nbytes = (nbytes + 31) & ~31
        assert self.off + nbytes <= self.limit, (
            f"{self.tag}/{name}: {self.off}+{nbytes} > {self.limit}")
        h = self.nc.alloc_sbuf_tensor_at(
            f"{self.tag}_{self.n}_{name}", shape, dtype, offset=self.off)
        self.n += 1
        self.off += nbytes
        return h


def t128(i):
    return slice(i * 128, (i + 1) * 128)


def build_program(cfg):
    S, D, H, DFF = cfg["S"], cfg["D"], cfg["H"], cfg["DFF"]
    G, SC, QC, EC = cfg["G"], cfg["SC"], cfg["QC"], cfg["EC"]
    HD = 128
    assert D == H * HD
    assert G == 1, "projection feeder assumes one head per group"
    NTS, ND, NF = S // 128, D // 128, DFF // 128
    NCH = S // SC
    QT = QC // 128
    NQC = S // QC
    NEC = D // EC
    FC = min(4, NF)
    scale = HD ** -0.5
    MAXPH = cfg.get("max_phase", 99)
    P0DBG = cfg.get("p0_dbg", "LTSW")

    nc = bass.Bass("TRN2", target_bir_lowering=False)

    d_x = nc.dram_tensor("x", [S, D], F32, kind="ExternalInput").ap()
    d_wqkv = nc.dram_tensor("wqkv", [H * 128, 3 * ND * 128], F32,
                            kind="ExternalInput").ap()
    d_wout = nc.dram_tensor("woutT", [D, D], F32, kind="ExternalInput").ap()
    d_w1p = nc.dram_tensor("w1p", [NF * 128, D], F32,
                           kind="ExternalInput").ap()
    d_w2t = nc.dram_tensor("w2t", [DFF, D], F32, kind="ExternalInput").ap()
    d_b1c = nc.dram_tensor("b1c", [128, NF], F32, kind="ExternalInput").ap()
    d_b2r = nc.dram_tensor("b2r", [1, D], F32, kind="ExternalInput").ap()
    d_ident = nc.dram_tensor("identc", [128, 128], F32,
                             kind="ExternalInput").ap()
    d_mask = nc.dram_tensor("maskneg", [128, 128], F32,
                            kind="ExternalInput").ap()
    d_out = nc.dram_tensor("out", [S, D], F32, kind="ExternalOutput").ap()
    d_xn = [nc.dram_tensor(f"xn_{p}", [NCH * 128, ND * SC], BF16).ap()
            for p in range(2)]
    d_y = [nc.dram_tensor(f"y_{p}", [D, S], BF16).ap() for p in range(2)]
    d_g = [nc.dram_tensor(f"g_{p}", [DFF, S], BF16).ap() for p in range(2)]

    SBW = min(S, 2048)
    ps_big = nc.alloc_psum_tensor("ps_big", [128, 2048], F32)
    ps_tr = nc.alloc_psum_tensor("ps_tr", [128, 1024], F32)
    ps_m = [nc.alloc_psum_tensor(f"ps_m{i}", [128, 512], F32) for i in (0, 1)]

    sch = Sched(nc)
    b_big = sch.buf()
    b_sb4 = [sch.buf() for _ in range(4)]
    sb_cnt = [0]
    b_tr = [sch.buf(), sch.buf()]
    b_m = [sch.buf(), sch.buf()]
    trc = [0]

    def tr_next():
        k = trc[0]
        trc[0] += 1
        b = k % 2
        return ps_tr[:, b * 512:b * 512 + 128], b_tr[b]

    SB_LIMIT = 229376 - 32
    ARENA_BASE = 17408
    ACC_BYTES = NTS * D * 4
    ACC_OFF = SB_LIMIT - ACC_BYTES
    pa = Arena(nc, ARENA_BASE, SB_LIMIT, "pers")
    ident = pa.alloc("ident", [128, 128], F32)
    mask = pa.alloc("mask", [128, 128], F32)
    ones = pa.alloc("ones", [1, 128], F32)
    b1c = pa.alloc("b1c", [128, max(NF, 8)], F32)
    thr = pa.alloc("thr", [128, 8], F32)
    sv = pa.alloc("sv", [128, 16], F32)
    cst = pa.alloc("cst", [128, 2], F32)
    st6 = pa.alloc("st6", [128, 6 * max((D + 511) // 512, 1)], F32)
    mv = pa.alloc("mv", [128, 2], F32)
    PERS_END = pa.off
    ppa = Arena(nc, PERS_END, SB_LIMIT, "pp")
    ppart = ppa.alloc("ppart",
                      [128, max(NF * D // 128, DFF * D // 16384, 128)], F32)
    pp2 = ppa.alloc("pp2", [128, 16], F32)
    PPW = min(512, D)
    ppw = [ppa.alloc(f"ppw{i}", [128, PPW], F32) for i in range(5)]
    PP_END = ppa.off
    assert PP_END < ACC_OFF

    b_ident, b_mask, b_ones, b_b1c = sch.buf(), sch.buf(), sch.buf(), sch.buf()
    b_thr = sch.buf()
    b_sv = sch.buf()
    b_cst = sch.buf()
    b_st6, b_mv = sch.buf(), sch.buf()
    b_ppart, b_pp2 = sch.buf(), sch.buf()
    b_ppw = [sch.buf() for _ in range(5)]
    b_ppwall = sch.buf()
    b_acc = None
    b_xn_dram = [[sch.buf() for _ in range(max(S // SC, 1))]
                 for _ in range(2)]
    b_y_dram = [sch.buf(), sch.buf()]
    b_g_dram = [sch.buf(), sch.buf()]
    b_out_dram = sch.buf()

    sch.op("sync", lambda e: e.dma_start(ident[:, :], d_ident),
           writes=[b_ident], dma=True)
    sch.op("sync", lambda e: e.dma_start(mask[:, :], d_mask),
           writes=[b_mask], dma=True)
    sch.op("sync", lambda e: e.dma_start(b1c[:, 0:NF], d_b1c),
           writes=[b_b1c], dma=True)
    sch.op("vector", lambda e: e.memset(ones[0:1, :], 1.0), writes=[b_ones])
    sch.op("vector", lambda e: e.memset(cst[:, 0:1], LN_EPS), writes=[b_cst])
    sch.op("vector", lambda e: e.memset(cst[:, 1:2], scale), writes=[b_cst])

    def split_psum(ps_ap, hi_ap, lo_ap, ps_buf, hi_buf, lo_buf):
        sch.op("scalar", lambda e, a=ps_ap, b=hi_ap: e.copy(b, a),
               reads=[ps_buf], writes=[hi_buf])
        sch.op("vector",
               lambda e, a=ps_ap, b=hi_ap, c=lo_ap:
               e.scalar_tensor_tensor(c, a, 1.0, b, ALU.mult, ALU.subtract),
               reads=[ps_buf, hi_buf], writes=[lo_buf])

    def split_psum_v(ps_ap, hi_ap, lo_ap, ps_buf, hi_buf, lo_buf):
        sch.op("vector", lambda e, a=ps_ap, b=hi_ap: e.tensor_copy(b, a),
               reads=[ps_buf], writes=[hi_buf])
        sch.op("vector",
               lambda e, a=ps_ap, b=hi_ap, c=lo_ap:
               e.scalar_tensor_tensor(c, a, 1.0, b, ALU.mult, ALU.subtract),
               reads=[ps_buf, hi_buf], writes=[lo_buf])

    def prepass_stream(d_w, rows):
        nblk = rows // 128
        npc = D // PPW
        pcw = PPW // 128
        chunks = [(r, q) for r in range(nblk) for q in range(npc)]
        state = {"i": 0}

        def step(k):
            i0 = state["i"]
            for idx in range(i0, min(i0 + k, len(chunks))):
                r, q = chunks[idx]
                i = idx % 5
                sch.op("sync",
                       lambda e, r=r, q=q, i=i: e.dma_start(
                           ppw[i][:, :], d_w[t128(r), q * PPW:(q + 1) * PPW]),
                       war=[b_ppw[i]], writes_nodep=[b_ppw[i]],
                       dma=True)
                c0 = r * npc + q
                sch.op("scalar",
                       lambda e, c0=c0, i=i: e.activation(
                           ppw[i][:, :], ppw[i][:, :], AF.Abs,
                           accum_out=ppart[:, c0:c0 + 1]),
                       reads=[b_ppw[i]], writes=[b_ppw[i], b_ppart])
            state["i"] = min(i0 + k, len(chunks))

        def flush():
            step(len(chunks))
        return step, flush, nblk * npc

    def prepass_fin(tot, thr_col, nelem):
        k1 = 1
        while tot // k1 > 128:
            k1 *= 2
        sch.op("vector",
               lambda e, tot=tot, k1=k1: e.tensor_reduce(
                   pp2[:, 0:k1],
                   ppart[:, 0:tot].rearrange("p (k c) -> p k c", k=k1),
                   AX.X, ALU.add, False),
               reads=[b_ppart], writes=[b_pp2])
        sch.op("vector",
               lambda e, k1=k1, thr_col=thr_col: e.tensor_reduce(
                   pp2[:, 8 + thr_col:9 + thr_col],
                   pp2[:, 0:k1], AX.X, ALU.add, False),
               reads=[b_pp2], writes=[b_pp2])
        ta, tb = tr_next()
        sch.op("tensor",
               lambda e, ta=ta, thr_col=thr_col: e.transpose(
                   ta[0:1, :], pp2[:, 8 + thr_col:9 + thr_col], ident[:, :]),
               reads=[b_pp2, b_ident], writes=[tb])
        sch.op("vector",
               lambda e, ta=ta, thr_col=thr_col: e.tensor_reduce(
                   pp2[0:1, 12 + thr_col:13 + thr_col],
                   ta[0:1, :], AX.X, ALU.add, False),
               reads=[tb], writes=[b_pp2])
        tc, tcb = tr_next()
        sch.op("tensor",
               lambda e, tc=tc, thr_col=thr_col: e.matmul(
                   tc[:, 0:1],
                   ones[0:1, 0:128], pp2[0:1, 12 + thr_col:13 + thr_col],
                   start=True, stop=True),
               reads=[b_pp2, b_ones], writes=[tcb])
        c = 0.7 / float(nelem)
        sch.op("vector",
               lambda e, tc=tc, thr_col=thr_col, c=c: e.tensor_scalar_mul(
                   thr[:, 2 * thr_col:2 * thr_col + 1], tc[:, 0:1], c),
               reads=[tcb], writes=[b_thr])
        sch.op("vector",
               lambda e, tc=tc, thr_col=thr_col, c=c: e.tensor_scalar_mul(
                   thr[:, 2 * thr_col + 1:2 * thr_col + 2], tc[:, 0:1], -c),
               reads=[tcb], writes=[b_thr])

    NOPP = cfg.get("no_prepass", False)
    if NOPP:
        pp1_step = pp1_flush = lambda *a: None
        tot1 = 0
    else:
        pp1_step, pp1_flush, tot1 = prepass_stream(d_w1p, NF * 128)

    def _cut(k):
        if MAXPH < k:
            sch.finalize()
            sch.emit()
            return True
        return False

    sch.set_barrier()
    p0 = Arena(nc, PP_END, SB_LIMIT, "p0")
    xt = [p0.alloc(f"xt{i}", [128, D], F32) for i in (0, 1)]
    xnt = p0.alloc("xnt", [128, D], F32)
    stg_h = [p0.alloc(f"xstg{p}", [128, ND * SC], BF16) for p in (0, 1)]
    stg = stg_h
    b_xt = [sch.buf(), sch.buf()]
    b_xnt = sch.buf()
    b_stg = [sch.buf(), sch.buf()]

    BNW = min(512, D)
    nbn = D // BNW
    for t in range(NTS):
        sl = t % 2
        sch.op("sync", lambda e, t=t, sl=sl: e.dma_start(
            xt[sl][:, :], d_x[t128(t), :]),
            writes=[b_xt[sl]], dma=True)
        if "L" not in P0DBG:
            sch.op("vector", lambda e, sl=sl: e.tensor_copy(
                xnt[:, :], xt[sl][:, :]), reads=[b_xt[sl]], writes=[b_xnt])
        for c in (range(nbn) if "L" in P0DBG else ()):
            sch.op("vector", lambda e, c=c, sl=sl: e.bn_stats(
                st6[:, 6 * c:6 * (c + 1)], xt[sl][:, BNW * c:BNW * (c + 1)]),
                reads=[b_xt[sl]], writes=[b_st6])
        if "L" not in P0DBG:
            pass
        else:
         sch.op("vector", lambda e: e.bn_aggr(mv[:, 0:2], st6[:, 0:6 * nbn]),
                reads=[b_st6], writes=[b_mv])
        if "L" in P0DBG:
         sch.op("scalar", lambda e: e.activation(
             sv[:, 6:7], mv[:, 1:2], AF.Sqrt, bias=cst[:, 0:1], scale=1.0),
             reads=[b_mv, b_cst], writes=[b_sv])
         sch.op("vector", lambda e: e.reciprocal(sv[:, 0:1], sv[:, 6:7]),
                reads=[b_sv], writes=[b_sv])
         sch.op("vector", lambda e: e.tensor_scalar_mul(
             sv[:, 1:2], mv[:, 0:1], -1.0), reads=[b_mv], writes=[b_sv])
         sch.op("vector", lambda e, sl=sl: e.tensor_scalar(
             xnt[:, :], xt[sl][:, :], sv[:, 1:2], sv[:, 0:1],
             ALU.add, ALU.mult),
             reads=[b_xt[sl], b_sv], writes=[b_xnt])
        ch, tl = t // (SC // 128), t % (SC // 128)
        for j0 in (range(0, ND, 4) if "T" in P0DBG else ()):
            ng = min(4, ND - j0)
            kk = trc[0]
            trc[0] += 1
            bb = kk % 2
            for jj in range(ng):
                sch.op("tensor", lambda e, j=j0 + jj, jj=jj, bb=bb:
                       e.transpose(
                           ps_tr[:, bb * 512 + jj * 128:
                                 bb * 512 + jj * 128 + 128],
                           xnt[:, t128(j)], ident[:, :]),
                       reads=[b_xnt, b_ident],
                       writes=[b_tr[bb]] if jj == 0 else [],
                       writes_nodep=[] if jj == 0 else [b_tr[bb]])
            if "S" in P0DBG:
                srcv = ps_tr[:, bb * 512:bb * 512 + ng * 128].rearrange(
                    "p (g c) -> p g c", c=128)
                off = j0 * SC + tl * 128
                dims = [[ND * SC, 128], [SC, ng], [1, 128]]
                dsth = bass.AP(stg_h[0], off, dims)
                dstl = bass.AP(stg_h[1], off, dims)
                sch.op("scalar", lambda e, srcv=srcv, dsth=dsth:
                       e.copy(dsth, srcv),
                       reads=[b_tr[bb]], writes=[b_stg[0]])
                sch.op("vector", lambda e, srcv=srcv, dsth=dsth, dstl=dstl:
                       e.scalar_tensor_tensor(dstl, srcv, 1.0, dsth,
                                              ALU.mult, ALU.subtract),
                       reads=[b_tr[bb], b_stg[0]], writes=[b_stg[1]])
        if tl == SC // 128 - 1 and "W" in P0DBG:
            for p in range(2):
                sch.op("scalar", lambda e, p=p, ch=ch: e.dma_start(
                    d_xn[p][t128(ch), :], stg[p][:, :]),
                    reads=[b_stg[p]], writes=[b_xn_dram[p][ch]], dma=True)
        pp1_step(128 // NTS + 1)

    if not NOPP:
        pp1_flush()
        prepass_fin(tot1, 0, DFF * D)
    else:
        sch.op("vector", lambda e: e.memset(thr[:, 0:4], 0.1),
               writes=[b_thr])
    if _cut(2):
        return nc

    sch.set_barrier()
    f1 = Arena(nc, PP_END, SB_LIMIT, "f1")
    xnf = [f1.alloc(f"xnf{p}", [128, NCH * ND * SC], BF16) for p in (0, 1)]
    w1s = [f1.alloc(f"w1s{i}", [128, ND * 128], F32) for i in (0, 1)]
    w1m = [f1.alloc(f"w1m{i}", [128, ND * 128], BF16) for i in (0, 1)]
    w1q = [f1.alloc(f"w1q{i}", [128, ND * 128], BF16) for i in (0, 1)]
    gt = [f1.alloc(f"gt{i}", [128, SC], F32) for i in (0, 1)]
    gst = [[f1.alloc(f"gst{i}{p}", [128, SC], BF16) for p in (0, 1)]
           for i in (0, 1)]
    b_xnf = [sch.buf(), sch.buf()]
    b_w1s = [sch.buf(), sch.buf()]
    b_w1m = [sch.buf(), sch.buf()]
    b_w1q = [sch.buf(), sch.buf()]
    b_gt = [sch.buf(), sch.buf()]
    b_gst = [[sch.buf(), sch.buf()], [sch.buf(), sch.buf()]]

    for p in range(2):
        for ch in range(NCH):
            sch.op("sync", lambda e, p=p, ch=ch: e.dma_start(
                xnf[p][:, ch * ND * SC:(ch + 1) * ND * SC],
                d_xn[p][t128(ch), :]),
                reads=[b_xn_dram[p][ch]], writes=[b_xnf[p]], dma=True)

    if NOPP:
        pp2_step = pp2_flush = lambda *a: None
        tot2 = 0
    else:
        pp2_step, pp2_flush, tot2 = prepass_stream(d_w2t, DFF)

    def w1_fetch_quant(ft):
        sl = ft % 2
        sch.op("sync", lambda e, ft=ft, sl=sl: e.dma_start(
            w1s[sl][:, :], d_w1p[t128(ft), :]),
            writes=[b_w1s[sl]], dma=True)
        sch.op("vector", lambda e, sl=sl: e.tensor_single_scalar(
            w1m[sl][:, :], w1s[sl][:, :], thr[:, 1:2], ALU.is_le),
            reads=[b_w1s[sl], b_thr], writes=[b_w1m[sl]])
        sch.op("vector", lambda e, sl=sl: e.scalar_tensor_tensor(
            w1q[sl][:, :], w1s[sl][:, :], thr[:, 0:1], w1m[sl][:, :],
            ALU.is_ge, ALU.subtract),
            reads=[b_w1s[sl], b_w1m[sl], b_thr], writes=[b_w1q[sl]])

    w1_fetch_quant(0)
    for ft in range(NF):
        sl = ft % 2
        if ft + 1 < NF:
            w1_fetch_quant(ft + 1)
        pp2_step(128 // NF + 2 if NF >= 32 else 64)
        for ch in range(NCH):
            pm = (ft * NCH + ch) % 2
            first = True
            for j in range(ND):
                for p in range(2):
                    base = ch * ND * SC + j * SC
                    sch.op("tensor", lambda e, j=j, p=p, pm=pm, base=base,
                           sl=sl, first=first, last=(j == ND - 1 and p == 1):
                           e.matmul(
                               ps_m[pm][:, 0:SC], w1q[sl][:, t128(j)],
                               xnf[p][:, base:base + SC],
                               start=first, stop=last),
                           reads=[b_w1q[sl], b_xnf[p]],
                           writes=[b_m[pm]] if first else [],
                           writes_nodep=[] if first else [b_m[pm]])
                    first = False
            gsl = (ft * NCH + ch) % 2
            sch.op("scalar", lambda e, ft=ft, pm=pm, gsl=gsl: e.activation(
                gt[gsl][:, :], ps_m[pm][:, 0:SC], AF.Gelu,
                bias=b1c[:, ft:ft + 1], scale=1.0),
                reads=[b_m[pm], b_b1c], writes=[b_gt[gsl]])
            sch.op("scalar", lambda e, ft=ft, pm=pm, gsl=gsl:
                   e.activation(gst[gsl][0][:, :], ps_m[pm][:, 0:SC],
                                AF.Gelu, bias=b1c[:, ft:ft + 1], scale=1.0),
                   reads=[b_m[pm], b_b1c], writes=[b_gst[gsl][0]])
            sch.op("vector", lambda e, gsl=gsl: e.scalar_tensor_tensor(
                gst[gsl][1][:, :], gt[gsl][:, :], 1.0, gst[gsl][0][:, :],
                ALU.mult, ALU.subtract),
                reads=[b_gt[gsl], b_gst[gsl][0]], writes=[b_gst[gsl][1]])
            for p in range(2):
                sch.op("scalar", lambda e, p=p, ft=ft, ch=ch, gsl=gsl:
                       e.dma_start(d_g[p][t128(ft), ch * SC:(ch + 1) * SC],
                                   gst[gsl][p][:, :]),
                       reads=[b_gst[gsl][p]], writes=[b_g_dram[p]], dma=True)

    if _cut(3):
        return nc


    if not NOPP:
        pp2_flush()
        prepass_fin(tot2, 1, DFF * D)

    sch.set_barrier()
    aa = Arena(nc, PERS_END, SB_LIMIT, "att")
    xna = [aa.alloc(f"xna{p}", [128, ND * SC], BF16) for p in (0, 1)]
    WCOLS = 3 * ND * 128
    PW = min(256, WCOLS)
    wqs = [aa.alloc(f"wqs{i}", [128, PW], F32) for i in (0, 1)]
    wsl = [[aa.alloc(f"wsl{par}_{p}", [128, G * WCOLS], BF16)
            for p in (0, 1)] for par in (0, 1)]
    qh = [[aa.alloc(f"q{pq}_{p}", [128, S], BF16) for p in (0, 1)]
          for pq in range(2)]
    kh = [[aa.alloc(f"k{pq}_{p}", [128, S], BF16) for p in (0, 1)]
          for pq in range(2)]
    vh = [[aa.alloc(f"v{pq}_{p}", [128, S], BF16) for p in (0, 1)]
          for pq in range(2)]
    vf = [aa.alloc("vf0", [128, SC], F32)]
    af32 = [aa.alloc(f"af32_{i}", [128, SBW], F32) for i in range(3)]
    ath_h = [aa.alloc(f"at_{p}", [128, NTS * QC], BF16) for p in (0, 1)]
    ath = ath_h
    yst = [[aa.alloc(f"yst{i}{p}", [128, QC], BF16) for p in (0, 1)]
           for i in (0,)]
    asv = aa.alloc("asv", [128, 32], F32)
    b_xna = [sch.buf(), sch.buf()]
    b_wqs = [sch.buf(), sch.buf()]
    b_wsl = [[sch.buf(), sch.buf()] for _ in (0, 1)]
    b_qh = [[sch.buf(), sch.buf()] for _ in range(2)]
    b_kh = [[sch.buf(), sch.buf()] for _ in range(2)]
    b_vh = [[sch.buf(), sch.buf()] for _ in range(2)]
    pmg = [0]
    b_vf = [sch.buf()]
    b_af32 = [sch.buf() for _ in range(3)]
    b_ath = [sch.buf(), sch.buf()]
    b_yst = [[sch.buf(), sch.buf()]]
    b_asv = [sch.buf(), sch.buf()]

    def fill_wgroup(grp):
        par = grp % 2
        pieces = [(o, min(PW, WCOLS - o)) for o in range(0, WCOLS, PW)]
        for h2 in range(G):
            head = grp * G + h2
            for pc, (po, pw_) in enumerate(pieces):
                wsl2 = (h2 * len(pieces) + pc) % 2
                sch.op("sync", lambda e, head=head, po=po, pw_=pw_, wsl2=wsl2:
                       e.dma_start(wqs[wsl2][:, 0:pw_],
                                   d_wqkv[t128(head), po:po + pw_]),
                       writes=[b_wqs[wsl2]], dma=True)
                co = h2 * WCOLS + po
                sch.op("gpsimd", lambda e, co=co, pw_=pw_, wsl2=wsl2, par=par:
                       e.tensor_copy(wsl[par][0][:, co:co + pw_],
                                     wqs[wsl2][:, 0:pw_]),
                       reads=[b_wqs[wsl2]], writes=[b_wsl[par][0]])
                sch.op("gpsimd", lambda e, co=co, pw_=pw_, wsl2=wsl2, par=par:
                       e.tensor_tensor(
                           wsl[par][1][:, co:co + pw_], wqs[wsl2][:, 0:pw_],
                           wsl[par][0][:, co:co + pw_], ALU.subtract),
                       reads=[b_wqs[wsl2], b_wsl[par][0]],
                       writes=[b_wsl[par][1]])

    fill_wgroup(0)
    for grp in range(H // G):

        par = grp % 2

        def mk_proj_tasks(g2):
            pq = g2 % 2
            tasks = []

            def wap(p, h2, tgt, j, pq=pq):
                co = h2 * WCOLS + (tgt * ND + j) * 128
                return wsl[pq][p][:, co:co + 128]

            for ch in range(NCH):
                def t_load(ch=ch):
                    for p in range(2):
                        sch.op("sync", lambda e, p=p, ch=ch: e.dma_start(
                            xna[p][:, :], d_xn[p][t128(ch), :]),
                            reads=[b_xn_dram[p][ch]], writes=[b_xna[p]],
                            dma=True)
                tasks.append(t_load)
                for tgt in range(3):
                    def t_grp(ch=ch, tgt=tgt, pq=pq, wap=wap):
                        pm = pmg[0] % 2
                        pmg[0] += 1
                        h2 = 0
                        first = True
                        for j in range(ND):
                            for (pw, px) in ((0, 0), (0, 1), (1, 0)):
                                last = (j == ND - 1 and (pw, px) == (1, 0))
                                sch.op("tensor",
                                       lambda e, pw=pw, px=px, tgt=tgt,
                                       j=j, pm=pm, first=first, last=last,
                                       wap=wap: e.matmul(
                                           ps_m[pm][:, 0:SC],
                                           wap(pw, 0, tgt, j),
                                           xna[px][:, j * SC:j * SC + SC],
                                           start=first, stop=last),
                                       reads=[b_wsl[pq][pw], b_xna[px]],
                                       writes=[b_m[pm]] if first else [],
                                       writes_nodep=[] if first
                                       else [b_m[pm]])
                                first = False
                        if tgt < 2:
                            dst = qh[pq] if tgt == 0 else kh[pq]
                            dbuf = b_qh[pq] if tgt == 0 else b_kh[pq]
                            split_psum(ps_m[pm][:, 0:SC],
                                       dst[0][:, ch * SC:(ch + 1) * SC],
                                       dst[1][:, ch * SC:(ch + 1) * SC],
                                       b_m[pm], dbuf[0], dbuf[1])
                        else:
                            vsl = 0
                            sch.op("vector", lambda e, pm=pm, vsl=vsl:
                                   e.tensor_copy(vf[vsl][:, :],
                                                 ps_m[pm][:, 0:SC]),
                                   reads=[b_m[pm]], writes=[b_vf[vsl]])
                            for tv0 in range(0, SC // 128, 4):
                                ngv = min(4, SC // 128 - tv0)
                                kk = trc[0]
                                trc[0] += 1
                                bb = kk % 2
                                for jj in range(ngv):
                                    sch.op("tensor",
                                           lambda e, tv=tv0 + jj, jj=jj,
                                           bb=bb, vsl=vsl: e.transpose(
                                               ps_tr[:, bb * 512 + jj * 128:
                                                     bb * 512 + jj * 128
                                                     + 128],
                                               vf[vsl][:, t128(tv)],
                                               ident[:, :]),
                                           reads=[b_vf[vsl], b_ident],
                                           writes=[b_tr[bb]] if jj == 0
                                           else [],
                                           writes_nodep=[] if jj == 0
                                           else [b_tr[bb]])
                                co = (ch * (SC // 128) + tv0) * 128
                                wv = ngv * 128
                                split_psum(ps_tr[:, bb * 512:bb * 512 + wv],
                                           vh[pq][0][:, co:co + wv],
                                           vh[pq][1][:, co:co + wv],
                                           b_tr[bb], b_vh[pq][0],
                                           b_vh[pq][1])
                    tasks.append(t_grp)
            return tasks

        if grp == 0:
            for t in mk_proj_tasks(0):
                t()
        if grp + 1 < H // G:
            fill_wgroup(grp + 1)
        feeder = mk_proj_tasks(grp + 1) if grp + 1 < H // G else []

        def feed():
            if feeder:
                feeder.pop(0)()

        for h2 in range(G):
            head = grp * G + h2
            for til in range(QT):
                for r in range(til + 1, QT):
                    for p in range(2):
                        off = r * QC + til * 128
                        dims = [[NTS * QC, 128], [QT * QC, NTS // QT],
                                [1, 128]]
                        ap = bass.AP(ath_h[p], off, dims)
                        sch.op("vector", lambda e, ap=ap: e.memset(ap, 0.0),
                               writes=[b_ath[p]])
            pending_tr = []
            pending_av = [None]

            def flush_tr():
                for emit in pending_tr:
                    emit()
                pending_tr.clear()

            for qc in range(NQC):
                KT = (qc + 1) * QT
                for ti in range(QT):
                    ig = qc * QT + ti
                    klen = (ig + 1) * 128
                    nm = (klen + 511) // 512
                    p2 = ti % 2
                    B = 16 * p2
                    sb = b_asv[p2]
                    asl = ig % 3
                    for m in range(nm):
                        w = min(512, klen - m * 512)
                        slot = sb_cnt[0] % 4
                        sb_cnt[0] += 1
                        for (pq, pk) in ((0, 0), (0, 1), (1, 0)):
                            first = (pq, pk) == (0, 0)
                            sch.op("tensor",
                                   lambda e, pq=pq, pk=pk, par=par,
                                   ig=ig, m=m,
                                   slot=slot, w=w, first=first: e.matmul(
                                       ps_big[:, slot * 512:slot * 512 + w],
                                       qh[par][pq][:, t128(ig)],
                                       kh[par][pk][:, m * 512:m * 512 + w],
                                       start=first,
                                       stop=(pq, pk) == (1, 0)),
                                   reads=[b_qh[par][pq], b_kh[par][pk]],
                                   writes=[b_sb4[slot]] if first else [],
                                   writes_nodep=[] if first else
                                   [b_sb4[slot]])
                        sch.op("vector", lambda e, m=m, w=w, slot=slot,
                               asl=asl: e.tensor_copy(
                                   af32[asl][:, m * 512:m * 512 + w],
                                   ps_big[:, slot * 512:slot * 512 + w]),
                               reads=[b_sb4[slot]], writes=[b_af32[asl]])
                        if m < nm - 1:
                            sch.op("vector", lambda e, m=m, w=w, B=B, asl=asl:
                                   e.reduce_max(
                                       asv[:, B + m:B + m + 1],
                                       af32[asl][:, m * 512:m * 512 + w],
                                       AX.X),
                                   reads=[b_af32[asl]], writes=[sb])
                    sch.op("vector", lambda e, ig=ig, klen=klen, asl=asl:
                           e.tensor_tensor(af32[asl][:, ig * 128:klen],
                                           af32[asl][:, ig * 128:klen],
                                           mask[:, :], ALU.add),
                           reads=[b_af32[asl], b_mask], writes=[b_af32[asl]])
                    m = nm - 1
                    w = klen - m * 512
                    sch.op("vector", lambda e, m=m, w=w, B=B, asl=asl:
                           e.reduce_max(asv[:, B + m:B + m + 1],
                                        af32[asl][:, m * 512:m * 512 + w],
                                        AX.X),
                           reads=[b_af32[asl]], writes=[sb])
                    sch.op("vector", lambda e, nm=nm, B=B: e.reduce_max(
                        asv[:, B + 4:B + 5], asv[:, B:B + nm], AX.X),
                        reads=[sb], writes=[sb])
                    sch.op("vector", lambda e, B=B: e.tensor_scalar_mul(
                        asv[:, B + 5:B + 6], asv[:, B + 4:B + 5], -scale),
                        reads=[sb], writes=[sb])
                    sch.op("scalar", lambda e, klen=klen, asl=asl, B=B:
                           e.activation(af32[asl][:, 0:klen],
                                        af32[asl][:, 0:klen], AF.Exp,
                                        bias=asv[:, B + 5:B + 6],
                                        scale=cst[:, 1:2],
                                        accum_out=asv[:, B + 6:B + 7]),
                           reads=[sb, b_cst],
                           writes=[b_af32[asl], sb])
                    sch.op("vector", lambda e, B=B: e.reciprocal(
                        asv[:, B + 7:B + 8], asv[:, B + 6:B + 7]),
                        reads=[sb], writes=[sb])
                    sch.op("vector", lambda e, klen=klen, asl=asl, B=B:
                           e.tensor_scalar_mul(af32[asl][:, 0:klen],
                                               af32[asl][:, 0:klen],
                                               asv[:, B + 7:B + 8]),
                           reads=[sb], writes=[b_af32[asl]])

                    def mk_tr(ig=ig, ti=ti, asl=asl):
                        def emit():
                            for j0 in range(0, ig + 1, 4):
                                ng = min(4, ig + 1 - j0)
                                k = trc[0]
                                trc[0] += 1
                                b = k % 2
                                for jj in range(ng):
                                    sch.op("tensor",
                                           lambda e, j=j0 + jj, jj=jj, b=b,
                                           asl=asl: e.transpose(
                                               ps_tr[:, b * 512 + jj * 128:
                                                     b * 512 + jj * 128 + 128],
                                               af32[asl][:, t128(j)],
                                               ident[:, :]),
                                           reads=[b_af32[asl], b_ident],
                                           writes=[b_tr[b]] if jj == 0 else [],
                                           writes_nodep=[] if jj == 0
                                           else [b_tr[b]])
                                srcv = ps_tr[:, b * 512:b * 512 + ng * 128
                                             ].rearrange("p (g c) -> p g c",
                                                         c=128)
                                off = j0 * QC + ti * 128
                                dims = [[NTS * QC, 128], [QC, ng], [1, 128]]
                                dsth = bass.AP(ath_h[0], off, dims)
                                dstl = bass.AP(ath_h[1], off, dims)
                                sch.op("scalar",
                                       lambda e, srcv=srcv, dsth=dsth:
                                       e.copy(dsth, srcv),
                                       reads=[b_tr[b]], writes=[b_ath[0]])
                                sch.op("vector",
                                       lambda e, srcv=srcv, dsth=dsth,
                                       dstl=dstl: e.scalar_tensor_tensor(
                                           dstl, srcv, 1.0, dsth,
                                           ALU.mult, ALU.subtract),
                                       reads=[b_tr[b], b_ath[0]],
                                       writes=[b_ath[1]])
                        return emit

                    if ti == 0 and pending_av[0] is not None:
                        flush_tr()
                        pending_av[0]()
                        pending_av[0] = None
                    pending_tr.append(mk_tr())
                    if len(pending_tr) > 2:
                        pending_tr.pop(0)()
                    feed()

                def mk_av(qc=qc, KT=KT, par=par, head=head):
                    def emit():
                        pm = pmg[0] % 2
                        pmg[0] += 1
                        first = True
                        for j in range(KT):
                            for (pv, pa) in ((0, 0), (0, 1), (1, 0)):
                                last = (j == KT - 1 and (pv, pa) == (1, 0))
                                sch.op("tensor",
                                       lambda e, pv=pv, pa=pa, par=par,
                                       j=j, pm=pm, first=first, last=last:
                                       e.matmul(
                                           ps_m[pm][:, 0:QC],
                                           vh[par][pv][:, t128(j)],
                                           ath[pa][:, j * QC:(j + 1) * QC],
                                           start=first, stop=last),
                                       reads=[b_vh[par][pv], b_ath[pa]],
                                       writes=[b_m[pm]] if first else [],
                                       writes_nodep=[] if first
                                       else [b_m[pm]])
                                first = False
                        ysl = 0
                        split_psum_v(ps_m[pm][:, 0:QC], yst[ysl][0][:, :],
                                     yst[ysl][1][:, :], b_m[pm],
                                     b_yst[ysl][0], b_yst[ysl][1])
                        for p in range(2):
                            sch.op("scalar",
                                   lambda e, p=p, head=head, qc=qc, ysl=ysl:
                                   e.dma_start(d_y[p][t128(head),
                                                      qc * QC:(qc + 1) * QC],
                                               yst[ysl][p][:, :]),
                                   reads=[b_yst[ysl][p]],
                                   writes=[b_y_dram[p]], dma=True)
                    return emit

                pending_av[0] = mk_av()
            flush_tr()
            if pending_av[0] is not None:
                pending_av[0]()
                pending_av[0] = None
            while feeder:
                feeder.pop(0)()

    if _cut(4):
        return nc

    sch.set_barrier()
    acc = nc.alloc_sbuf_tensor_at("acc", [128, NTS * D], F32, offset=ACC_OFF)
    op = Arena(nc, PERS_END, ACC_OFF, "op")
    SG = min(2, NTS)
    ysl_ = [[op.alloc(f"ysl{j}_{p}", [128, SG * 128], BF16) for p in (0, 1)]
            for j in range(ND)]
    wos = [op.alloc(f"wos{i}", [128, EC], F32) for i in (0, 1)]
    wol = [[op.alloc(f"wol{j}_{p}", [128, EC], BF16) for p in (0, 1)]
           for j in range(ND)]
    xot = [op.alloc(f"xot{i}", [128, EC], F32) for i in (0, 1)]
    b_acc = [sch.buf() for _ in range(NTS)]
    b_ysl = [[sch.buf(), sch.buf()] for _ in range(ND)]
    b_wos = [sch.buf(), sch.buf()]
    b_wol = [[sch.buf(), sch.buf()] for _ in range(ND)]
    b_xot = [sch.buf(), sch.buf()]

    for ec in range(NEC):
        for j in range(ND):
            wsl2 = j % 2
            sch.op("sync", lambda e, j=j, ec=ec, wsl2=wsl2: e.dma_start(
                wos[wsl2][:, :], d_wout[t128(j), ec * EC:(ec + 1) * EC]),
                writes=[b_wos[wsl2]], dma=True)
            sch.op("scalar", lambda e, j=j, wsl2=wsl2: e.copy(
                wol[j][0][:, :], wos[wsl2][:, :]),
                reads=[b_wos[wsl2]], writes=[b_wol[j][0]])
            sch.op("vector", lambda e, j=j, wsl2=wsl2:
                   e.scalar_tensor_tensor(
                       wol[j][1][:, :], wos[wsl2][:, :], 1.0,
                       wol[j][0][:, :], ALU.mult, ALU.subtract),
                   reads=[b_wos[wsl2], b_wol[j][0]], writes=[b_wol[j][1]])
        for sg in range(NTS // SG):
            for j in range(ND):
                for p in range(2):
                    sch.op("sync" if p == 0 else "scalar",
                           lambda e, j=j, p=p, sg=sg: e.dma_start(
                        ysl_[j][p][:, :],
                        d_y[p][t128(j), sg * SG * 128:(sg + 1) * SG * 128]),
                        reads=[b_y_dram[p]], writes=[b_ysl[j][p]], dma=True)
            for st2 in range(SG):
                st = sg * SG + st2
                pm = (ec * NTS + sg * SG + st2) % 2
                first = True
                for j in range(ND):
                    for (py, pw) in ((0, 0), (0, 1), (1, 0)):
                        last = (j == ND - 1 and (py, pw) == (1, 0))
                        sch.op("tensor",
                               lambda e, py=py, pw=pw, j=j, st2=st2, pm=pm,
                               first=first, last=last: e.matmul(
                                   ps_m[pm][:, 0:EC],
                                   ysl_[j][py][:, t128(st2)],
                                   wol[j][pw][:, :],
                                   start=first, stop=last),
                               reads=[b_ysl[j][py], b_wol[j][pw]],
                               writes=[b_m[pm]] if first else [],
                               writes_nodep=[] if first else [b_m[pm]])
                        first = False
                xsl = st2 % 2
                sch.op("scalar", lambda e, st=st, ec=ec, xsl=xsl: e.dma_start(
                    xot[xsl][:, :], d_x[t128(st), ec * EC:(ec + 1) * EC]),
                    reads=[b_xot[xsl]], writes=[b_xot[xsl]], dma=True)
                sch.op("vector", lambda e, st=st, ec=ec, pm=pm, xsl=xsl:
                       e.scalar_tensor_tensor(
                           acc[:, st * D + ec * EC:st * D + (ec + 1) * EC],
                           ps_m[pm][:, 0:EC], 1.0, xot[xsl][:, :],
                           ALU.mult, ALU.add),
                       reads=[b_m[pm], b_xot[xsl]], writes=[b_acc[st]])

    if _cut(5):
        return nc
    sch.set_barrier()
    f2 = Arena(nc, PERS_END, ACC_OFF, "f2")
    FCn = NF // FC
    gsl_ = [[f2.alloc(f"gsl{f}_{p}", [128, S], BF16) for p in (0, 1)]
            for f in range(FC)]
    w2s = [f2.alloc("w2s0", [128, D], F32)]
    w2m = [f2.alloc("w2m0", [128, D], BF16)]
    w2q = [f2.alloc(f"w2q{f}", [128, D], BF16) for f in range(2 * FC)]
    b_gsl = [[[sch.buf(), sch.buf()] for _ in range(2)]
             for _ in range(FC)]
    b_w2s = [sch.buf()]
    b_w2m = [sch.buf()]
    b_w2q = [sch.buf() for _ in range(2 * FC)]

    def w2_fetch_quant(fc):
        cpar = fc % 2
        for f in range(FC):
            ft = fc * FC + f
            fq = cpar * FC + f
            sch.op("sync", lambda e, ft=ft: e.dma_start(
                w2s[0][:, :], d_w2t[t128(ft), :]),
                reads=[b_w2s[0]], writes=[b_w2s[0]], dma=True)
            sch.op("vector", lambda e: e.tensor_single_scalar(
                w2m[0][:, :], w2s[0][:, :], thr[:, 3:4], ALU.is_le),
                reads=[b_w2s[0], b_thr], writes=[b_w2m[0]])
            sch.op("vector", lambda e, fq=fq:
                   e.scalar_tensor_tensor(
                       w2q[fq][:, :], w2s[0][:, :], thr[:, 2:3],
                       w2m[0][:, :], ALU.is_ge, ALU.subtract),
                   reads=[b_w2s[0], b_w2m[0], b_thr],
                   writes=[b_w2q[fq]])

    HS = S // 2

    def g_load(fc, half):
        for f in range(FC):
            ft = fc * FC + f
            for p in range(2):
                sch.op("scalar", lambda e, ft=ft, f=f, p=p, half=half:
                       e.dma_start(
                           gsl_[f][p][:, half * HS:(half + 1) * HS],
                           d_g[p][t128(ft), half * HS:(half + 1) * HS]),
                       reads=[b_g_dram[p], b_gsl[f][half][p]],
                       writes=[b_gsl[f][half][p]], dma=True)

    w2_fetch_quant(0)
    g_load(0, 0)
    g_load(0, 1)
    for fc in range(FCn):
        cpar = fc % 2
        if fc + 1 < FCn:
            w2_fetch_quant(fc + 1)
        for st in range(NTS):
            half = (st * 128) // HS
            if st == NTS // 2 and fc + 1 < FCn:
                g_load(fc + 1, 0)
            for ec in range(NEC):
                pm = (st * NEC + ec) % 2
                first = True
                for f in range(FC):
                    fq = cpar * FC + f
                    for p in range(2):
                        last = (f == FC - 1 and p == 1)
                        sch.op("tensor",
                               lambda e, f=f, fq=fq, p=p, st=st, ec=ec,
                               pm=pm, first=first, last=last: e.matmul(
                                   ps_m[pm][:, 0:EC],
                                   gsl_[f][p][:, t128(st)],
                                   w2q[fq][:, ec * EC:(ec + 1) * EC],
                                   start=first, stop=last),
                               reads=[b_gsl[f][half][p], b_w2q[fq]],
                               writes=[b_m[pm]] if first else [],
                               writes_nodep=[] if first else [b_m[pm]])
                        first = False
                sch.op("vector", lambda e, st=st, ec=ec, pm=pm:
                       e.scalar_tensor_tensor(
                           acc[:, st * D + ec * EC:st * D + (ec + 1) * EC],
                           ps_m[pm][:, 0:EC], 1.0,
                           acc[:, st * D + ec * EC:st * D + (ec + 1) * EC],
                           ALU.mult, ALU.add),
                       reads=[b_m[pm]], writes=[b_acc[st]])
        if fc + 1 < FCn:
            g_load(fc + 1, 1)

    if _cut(6):
        return nc
    sch.set_barrier()
    ea = Arena(nc, PERS_END, ACC_OFF, "ea")
    b2row = ea.alloc("b2row", [1, D], F32)
    b2b = ea.alloc("b2b", [128, D], F32)
    ost = [ea.alloc(f"ost{i}", [128, EC], F32) for i in (0, 1)]
    b_b2row, b_b2b = sch.buf(), sch.buf()
    b_ost = [sch.buf(), sch.buf()]

    sch.op("sync", lambda e: e.dma_start(b2row[0:1, :], d_b2r),
           writes=[b_b2row], dma=True)
    for ec in range(NEC):
        pm = ec % 2
        sch.op("tensor", lambda e, ec=ec, pm=pm: e.matmul(
            ps_m[pm][:, 0:EC], ones[0:1, 0:128],
            b2row[0:1, ec * EC:(ec + 1) * EC], start=True, stop=True),
            reads=[b_b2row, b_ones], writes=[b_m[pm]])
        sch.op("vector", lambda e, ec=ec, pm=pm: e.tensor_copy(
            b2b[:, ec * EC:(ec + 1) * EC], ps_m[pm][:, 0:EC]),
            reads=[b_m[pm]], writes=[b_b2b])
    for st in range(NTS):
        for ec in range(NEC):
            osl = (st * NEC + ec) % 2
            sch.op("vector", lambda e, st=st, ec=ec, osl=osl:
                   e.scalar_tensor_tensor(
                       ost[osl][:, :],
                       acc[:, st * D + ec * EC:st * D + (ec + 1) * EC],
                       1.0, b2b[:, ec * EC:(ec + 1) * EC],
                       ALU.mult, ALU.add),
                   reads=[b_acc[st], b_b2b, b_ost[osl]], writes=[b_ost[osl]])
            sch.op("scalar", lambda e, st=st, ec=ec, osl=osl: e.dma_start(
                d_out[t128(st), ec * EC:(ec + 1) * EC], ost[osl][:, :]),
                reads=[b_ost[osl]], writes=[b_out_dram], dma=True)

    sch.finalize()
    sch.emit()
    return nc


def prep_weights(cfg, w_qkv, w_out, w1, b1, w2, b2):
    D, H, DFF = cfg["D"], cfg["H"], cfg["DFF"]
    ND, NF = D // 128, DFF // 128
    w4 = np.asarray(w_qkv, np.float32).reshape(3, H, 128, ND, 128)
    wqkv = np.ascontiguousarray(np.transpose(w4, (1, 4, 0, 3, 2))
                                ).reshape(H * 128, 3 * ND * 128)
    woutT = np.ascontiguousarray(np.asarray(w_out, np.float32).T)
    w1r = np.asarray(w1, np.float32).reshape(NF, 128, ND, 128)
    w1p = np.ascontiguousarray(np.transpose(w1r, (0, 3, 2, 1))
                               ).reshape(NF * 128, D)
    w2t = np.ascontiguousarray(np.asarray(w2, np.float32).T)
    b1c = np.ascontiguousarray(np.asarray(b1, np.float32).reshape(NF, 128).T)
    b2r = np.ascontiguousarray(np.asarray(b2, np.float32).reshape(1, D))
    identc = np.eye(128, dtype=np.float32)
    idx = np.arange(128)
    maskneg = np.where(idx[None, :] <= idx[:, None], 0.0, NEG_BIG
                       ).astype(np.float32)
    return dict(wqkv=wqkv, woutT=woutT, w1p=w1p, w2t=w2t, b1c=b1c, b2r=b2r,
                identc=identc, maskneg=maskneg)


def kernel(L0_attn_x, L0_attn_w_qkv, L0_attn_w_out, L0_ffn_w1, L0_ffn_b1,
           L0_ffn_w2, L0_ffn_b2):
    cfg = default_cfg()
    n = cfg["n_cores"]
    x = np.ascontiguousarray(np.asarray(L0_attn_x, np.float32))
    wd = prep_weights(cfg, L0_attn_w_qkv, L0_attn_w_out, L0_ffn_w1,
                      L0_ffn_b1, L0_ffn_w2, L0_ffn_b2)
    nc = build_program(cfg)
    in_maps = [dict(x=x[c], **wd) for c in range(n)]
    res = run_bass_kernel_spmd(nc, in_maps, core_ids=list(range(n)))
    return np.stack([np.asarray(r["out"], np.float32)
                     for r in res.results], axis=0)
